# version 10
# baseline (speedup 1.0000x reference)
import numpy as np
import ml_dtypes
import concourse.bass as bass
import concourse.mybir as mybir
from concourse.bass_utils import run_bass_kernel_spmd

F32 = mybir.dt.float32
BF16 = mybir.dt.bfloat16
AF = mybir.ActivationFunctionType
ALU = mybir.AluOpType
AXX = mybir.AxisListType.X

B, S, E = 4, 4096, 1024
DEBUG = False
H, DH = 16, 64
I = 2816
NB_T = 32
NQB = 16
EPS = 1e-5


class Plan:
    ENGS = ("sync", "tensor", "vector", "scalar", "gpsimd")

    def __init__(self):
        self.items = {e: [] for e in self.ENGS}
        self.cnt = {}

    def c(self, sem):
        return self.cnt.get(sem, 0)

    def emit(self, eng, fn, waits=(), inc=None, inc_n=1):
        if inc is not None:
            self.cnt[inc] = self.cnt.get(inc, 0) + inc_n
        ws = {}
        for s, v in waits:
            if v > 0:
                ws[s] = max(ws.get(s, 0), v)
        self.items[eng].append((fn, list(ws.items()), inc, inc_n))


def _sl(t, off, dims):
    a = t.ap()
    return bass.AP(a.tensor, a.offset + off, dims)


def build_core():
    nc = bass.Bass("TRN2", target_bir_lowering=False, dynamic_dma_scratch_size=2048)

    d_x = nc.dram_tensor("x", [S, E], F32, kind="ExternalInput").ap()
    d_wq = nc.dram_tensor("wq", [E, E], F32, kind="ExternalInput").ap()
    d_wkv = nc.dram_tensor("wkv", [E, 2 * E], F32, kind="ExternalInput").ap()
    d_wo = nc.dram_tensor("wo", [E, E], F32, kind="ExternalInput").ap()
    d_wg = nc.dram_tensor("wg", [I, E], F32, kind="ExternalInput").ap()
    d_wu = nc.dram_tensor("wu", [I, E], F32, kind="ExternalInput").ap()
    d_wd = nc.dram_tensor("wd", [E, I], F32, kind="ExternalInput").ap()
    d_bq = nc.dram_tensor("bq", [128, 8], F32, kind="ExternalInput").ap()
    d_bkv = nc.dram_tensor("bkv", [128, 16], F32, kind="ExternalInput").ap()
    d_bvb = nc.dram_tensor("bvb", [128, E], F32, kind="ExternalInput").ap()
    d_bob = nc.dram_tensor("bob", [128, E], F32, kind="ExternalInput").ap()
    d_chi = nc.dram_tensor("chi", [128, S], F32, kind="ExternalInput").ap()
    d_clo = nc.dram_tensor("clo", [128, S], F32, kind="ExternalInput").ap()
    d_shi = nc.dram_tensor("shi", [128, S], F32, kind="ExternalInput").ap()
    d_slo = nc.dram_tensor("slo", [128, S], F32, kind="ExternalInput").ap()
    d_tri = nc.dram_tensor("tri", [128, 128], BF16, kind="ExternalInput").ap()
    d_pm = nc.dram_tensor("pm", [128, 128], BF16, kind="ExternalInput").ap()
    d_out = nc.dram_tensor("out", [2048, E], F32, kind="ExternalOutput").ap()
    if DEBUG:
        d_dbg_lnxT = nc.dram_tensor("dbg_lnxT", [128, 8 * S], BF16,
                                    kind="ExternalOutput").ap()
        d_dbg_qT = nc.dram_tensor("dbg_qT", [128, 4 * 2048], BF16,
                                  kind="ExternalOutput").ap()
        d_dbg_kT = nc.dram_tensor("dbg_kT", [128, 4 * S], BF16,
                                  kind="ExternalOutput").ap()
        d_dbg_vp = nc.dram_tensor("dbg_vp", [128, NB_T * 520], BF16,
                                  kind="ExternalOutput").ap()
        d_dbg_bh = nc.dram_tensor("dbg_bh", [128, 16], F32,
                                  kind="ExternalOutput").ap()
        d_dbg_ao = nc.dram_tensor("dbg_ao", [128, 16384], BF16,
                                  kind="ExternalOutput").ap()
        d_dbg_x2 = nc.dram_tensor("dbg_x2", [128, 1024], F32,
                                  kind="ExternalOutput").ap()
        d_dbg_xq = nc.dram_tensor("dbg_xq", [128, 1024], BF16,
                                  kind="ExternalOutput").ap()
        d_dbg_hqT = nc.dram_tensor("dbg_hqT", [128, 2816], BF16,
                                   kind="ExternalOutput").ap()
        d_dbg_wgq = nc.dram_tensor("dbg_wgq", [128, 8 * I], BF16,
                                   kind="ExternalOutput").ap()
        d_dbg_wuq = nc.dram_tensor("dbg_wuq", [128, 8 * I], BF16,
                                   kind="ExternalOutput").ap()
        d_dbg_wdq = nc.dram_tensor("dbg_wdq", [128, 22 * 1024], BF16,
                                   kind="ExternalOutput").ap()
        d_dbg_ln2 = nc.dram_tensor("dbg_ln2", [128, 1024], F32,
                                   kind="ExternalOutput").ap()
        d_dbg_xqT = nc.dram_tensor("dbg_xqT", [128, 1024], BF16,
                                   kind="ExternalOutput").ap()
        d_dbg_bh0 = nc.dram_tensor("dbg_bh0", [128, 16], F32,
                                   kind="ExternalOutput").ap()
        d_dbg_nrm = nc.dram_tensor("dbg_nrm", [128, 16], F32,
                                   kind="ExternalOutput").ap()
        d_dbg_brow = nc.dram_tensor("dbg_brow", [128, 24], F32,
                                    kind="ExternalOutput").ap()
        d_dbg_ns = nc.dram_tensor("dbg_ns", [128, 128], F32,
                                  kind="ExternalOutput").ap()
        d_dbg_sqb = nc.dram_tensor("dbg_sqb", [128, 4096], BF16,
                                   kind="ExternalOutput").ap()
        d_dbg_ps6 = nc.dram_tensor("dbg_ps6", [128, 16], F32,
                                   kind="ExternalOutput").ap()

    SBUF_TOT = nc.sbuf_top
    off = [((nc.sbuf_base + 31) // 32) * 32]

    def alloc(name, shape, dt, at=None):
        nbytes = shape[1] * (2 if dt == BF16 else 4)
        if at is None:
            at = off[0]
            off[0] = at + ((nbytes + 31) // 32) * 32
            assert off[0] <= SBUF_TOT, f"SBUF overflow at {name}: {off[0]}"
        else:
            assert at + nbytes <= SBUF_TOT, f"SBUF overflow at {name}"
        return nc.alloc_sbuf_tensor_at(name, shape, dt, offset=at)

    tri = alloc("tri", [128, 128], BF16)
    pm = alloc("pm", [128, 128], BF16)
    ones_f = alloc("ones_f", [128, 64], F32)
    onesblk = alloc("onesblk", [128, 2], BF16)
    epsap = alloc("epsap", [128, 1], F32)
    bq_s = alloc("bq_s", [128, 8], F32)
    bkv_s = alloc("bkv_s", [128, 16], F32)
    biasH = alloc("biasH", [128, 16], F32)
    nrm_sc = alloc("nrm_sc", [128, 16], F32)
    brow = alloc("brow", [128, 24], F32)
    fmb = alloc("fmb", [128, 64], F32)
    fsc = alloc("fsc", [128, 64], F32)
    frc = alloc("frc", [128, 64], F32)
    dbgnrm = alloc("dbgnrm", [128, 16], F32)
    attnoutT = alloc("attnoutT", [128, 16384], BF16)
    base_after_const = off[0]

    lnxT = alloc("lnxT", [128, 8 * S], BF16)
    kT = alloc("kT", [128, 4 * S], BF16)
    qT = alloc("qT", [128, 4 * 2048], BF16)
    vpad = alloc("vpad", [128, NB_T * 520], BF16)
    y_base = off[0]
    prA = alloc("prA", [128, 1024], BF16)
    prB = alloc("prB", [128, 1024], BF16)
    rb_sb = alloc("rb_sb", [128, 512], F32)
    tmpB = alloc("tmpB", [128, 512], BF16)
    rrow = alloc("rrow", [128, 512], F32)
    corrA = alloc("corrA", [128, 512], F32)
    corrB = alloc("corrB", [128, 512], F32)
    kswp = alloc("kswp", [128, 512], F32)
    ropeA = alloc("ropeA", [128, 512], F32)
    tabhi = alloc("tabhi", [128, 512], F32)
    tablo = alloc("tablo", [128, 512], F32)
    tabshi = alloc("tabshi", [128, 512], F32)
    tabslo = alloc("tabslo", [128, 512], F32)
    kstg = alloc("kstg", [128, 1024], F32)
    sqb = alloc("sqb", [128, 512], BF16)
    wst = alloc("wst", [128, 128], F32)
    wbfa = alloc("wbfa", [128, 1024], BF16)
    wbfb = alloc("wbfb", [128, 1024], BF16)
    wvbf = alloc("wvbf", [128, 2048], BF16)
    bvb_s = alloc("bvb_s", [128, 512], F32)
    assert off[0] <= SBUF_TOT, f"phase2 SBUF over: {off[0]}"
    xo = [y_base]

    def xalloc(name, shape, dt):
        nbytes = shape[1] * (2 if dt == BF16 else 4)
        at = xo[0]
        xo[0] = at + ((nbytes + 31) // 32) * 32
        assert xo[0] <= off[0], f"X overlay over Y at {name}"
        return nc.alloc_sbuf_tensor_at(name, shape, dt, offset=at)

    lnx_bf = xalloc("lnx_bf", [128, 2048], BF16)
    xs0 = xalloc("xs0", [128, 1024], F32)
    xs1 = xalloc("xs1", [128, 1024], F32)
    sqf = xalloc("sqf", [128, 1024], F32)
    sqf2 = xalloc("sqf2", [128, 1024], F32)
    stats = xalloc("stats", [128, 16], F32)

    fo = [base_after_const]

    def falloc(name, shape, dt):
        nbytes = shape[1] * (2 if dt == BF16 else 4)
        at = fo[0]
        fo[0] = at + ((nbytes + 31) // 32) * 32
        assert fo[0] <= SBUF_TOT, f"FFN SBUF over at {name}: {fo[0]}"
        return nc.alloc_sbuf_tensor_at(name, shape, dt, offset=at)

    ft1 = falloc("ft1", [128, 1024], F32)
    ft2 = falloc("ft2", [128, 1024], F32)
    fc1 = falloc("fc1", [128, 1024], BF16)
    fc2 = falloc("fc2", [128, 1024], BF16)
    fc3 = falloc("fc3", [128, 1024], BF16)
    fqb = falloc("fqb", [128, 1024], BF16)
    sgb = falloc("sgb", [128, 1024], BF16)
    wgqT = falloc("wgqT", [128, 8 * I], BF16)
    assert fo[0] - base_after_const <= 8 * S * 2, "wg region exceeds lnxT"
    wuqT = falloc("wuqT", [128, 8 * I], BF16)
    wdqT = falloc("wdqT", [128, 22 * 1024], BF16)
    wo_bf = falloc("wo_bf", [128, 8 * 1024], BF16)
    bob_s = falloc("bob_s", [128, E], F32)
    xres = falloc("xres", [128, 1024], F32)
    x2 = falloc("x2", [128, 1024], F32)
    hqT = falloc("hqT", [128, 2816], BF16)
    xqT = falloc("xqT", [128, 1024], BF16)
    fstats = nrm_sc
    assert fo[0] <= SBUF_TOT, f"FFN SBUF over: {fo[0]}"

    ps = [nc.alloc_psum_tensor(f"p{i}", [128, 512], F32) for i in range(8)]

    pl = Plan()
    SEMS = ["xd", "gd", "lt", "tp", "v", "s", "t", "g", "out"]

    def emit_quant(src_t, W, dst_t, ay, c1, c2, c3, mb, sc, rc, wait_src=(),
                   part="all"):
        nb = W // 16
        nbp = max(nb, 64)
        assert src_t.ap().ap[-1][1] * src_t.ap().ap[-2][1] >= nbp * 16 or True
        a = src_t.ap()
        pst = a.ap[0][0]
        src = bass.AP(a.tensor, a.offset, [[pst, 128], [1, W]])
        src3 = bass.AP(a.tensor, a.offset, [[pst, 128], [16, nbp], [1, 16]])
        d = dst_t.ap()
        dst = bass.AP(d.tensor, d.offset, [[d.ap[0][0], 128], [1, W]])
        mbv = _sl(mb, 0, [[mb.ap().ap[0][0], 128], [1, nbp]])
        scv = _sl(sc, 0, [[sc.ap().ap[0][0], 128], [1, nbp]])
        rcv = _sl(rc, 0, [[rc.ap().ap[0][0], 128], [1, nbp]])
        rbb = _sl(rc, 0, [[rc.ap().ap[0][0], 128], [1, nb], [0, 16]])
        scb = _sl(sc, 0, [[sc.ap().ap[0][0], 128], [1, nb], [0, 16]])
        ayp = _sl(ay, 0, [[ay.ap().ap[0][0], 128], [1, W]])
        c1p = _sl(c1, 0, [[c1.ap().ap[0][0], 128], [1, W]])
        c2p = _sl(c2, 0, [[c2.ap().ap[0][0], 128], [1, W]])
        c3p = _sl(c3, 0, [[c3.ap().ap[0][0], 128], [1, W]])

        sgp = _sl(sgb, 0, [[sgb.ap().ap[0][0], 128], [1, W]])
        if part in ("all", "A"):
            pl.emit("scalar", lambda e: e.activation(ayp, src, AF.Abs),
                    waits=list(wait_src) + [("v", pl.c("v"))], inc="s")
            pl.emit("scalar", lambda e: e.activation(sgp, src, AF.Sign),
                    waits=[("v", pl.c("_sgbfree"))], inc="s")
            pl.cnt["_qA"] = pl.c("s")
            if part == "A":
                return
        pl.emit("vector", lambda e: e.tensor_reduce(
            mbv, src3, axis=AXX, op=ALU.max, apply_absolute_value=True),
            waits=wait_src, inc="v")
        pl.cnt["_qRed"] = pl.c("v")
        pl.emit("vector", lambda e: e.tensor_scalar(scv, mbv, 1.0 / 6.0, 1e-12,
                                                    ALU.mult, ALU.max))
        pl.emit("vector", lambda e: e.reciprocal(rcv, scv))
        pl.emit("vector", lambda e: e.tensor_tensor(ayp, ayp, rbb, op=ALU.mult),
                waits=[("s", pl.c("_qA"))])
        pl.emit("vector", lambda e: e.tensor_scalar(c1p, ayp, 0.25, None, ALU.is_gt))
        pl.emit("vector", lambda e: e.tensor_scalar(c2p, ayp, 0.75, None, ALU.is_gt))
        pl.emit("vector", lambda e: e.tensor_tensor(c1p, c1p, c2p, op=ALU.add))
        pl.emit("vector", lambda e: e.tensor_scalar(c2p, ayp, 1.25, None, ALU.is_gt))
        pl.emit("vector", lambda e: e.tensor_scalar(c3p, ayp, 1.75, None, ALU.is_gt))
        pl.emit("vector", lambda e: e.tensor_tensor(c2p, c2p, c3p, op=ALU.add))
        pl.emit("vector", lambda e: e.tensor_tensor(c1p, c1p, c2p, op=ALU.add))
        pl.emit("vector", lambda e: e.tensor_scalar(c2p, ayp, 2.5, None, ALU.is_gt))
        pl.emit("vector", lambda e: e.tensor_scalar(c3p, ayp, 3.5, None, ALU.is_gt))
        pl.emit("vector", lambda e: e.tensor_tensor(c2p, c2p, c3p, op=ALU.add))
        pl.emit("vector", lambda e: e.scalar_tensor_tensor(c2p, c1p, 0.5, c2p,
                                                           op0=ALU.mult, op1=ALU.add))
        pl.emit("vector", lambda e: e.tensor_scalar(c3p, ayp, 5.0, None, ALU.is_gt))
        pl.emit("vector", lambda e: e.scalar_tensor_tensor(c1p, c3p, 2.0, c2p,
                                                           op0=ALU.mult, op1=ALU.add))
        pl.emit("vector", lambda e: e.tensor_tensor(c1p, c1p, sgp, op=ALU.mult),
                waits=[("s", pl.c("_qA"))], inc="v")
        pl.cnt["_sgbfree"] = pl.c("v")
        pl.emit("vector", lambda e: e.tensor_tensor(dst, c1p, scb, op=ALU.mult),
                waits=[("tp", pl.c("_fqbxd"))], inc="v")

    for t, dsrc in ((tri, d_tri), (pm, d_pm), (bq_s, d_bq), (bkv_s, d_bkv)):
        pl.emit("sync", lambda e, t=t, dd=dsrc: e.dma_start(out=t[:], in_=dd),
                inc="xd", inc_n=16)
    pl.emit("vector", lambda e: e.memset(ones_f[:], 1.0), inc="v")
    pl.emit("vector", lambda e: e.memset(epsap[:], EPS), inc="v")
    pl.emit("vector", lambda e: e.memset(onesblk[:], 0.0), inc="v")
    pl.emit("vector", lambda e: e.memset(onesblk[0:64, 0:1], 1.0), inc="v")
    pl.emit("vector", lambda e: e.memset(onesblk[64:128, 1:2], 1.0), inc="v")
    pl.emit("vector", lambda e: e.memset(nrm_sc[:], -777.0), inc="v")
    v_consts = pl.c("v")
    xd_consts = pl.c("xd")

    lnv = {}
    for c in range(NB_T):
        p = c % 2
        b = 8 * p
        xbuf = xs0 if p == 0 else xs1
        sqfp = sqf if p == 0 else sqf2
        lboff = p * 1024
        pl.emit("sync", lambda e, c=c, xb=xbuf: e.dma_start(
            out=xb[:], in_=d_x[c * 128:(c + 1) * 128, :]),
            waits=[("v", lnv.get(c - 2, 0))], inc="xd", inc_n=16)
        xd_c = pl.c("xd")
        pl.emit("vector", lambda e, xb=xbuf, b=b: e.reduce_sum(
            stats[:, b:b + 1], xb[:], axis=AXX),
            waits=[("xd", xd_c), ("s", pl.c(f"_lnmul{p}"))], inc="v")
        pl.emit("scalar", lambda e, xb=xbuf, sq=sqfp: e.activation(
            sq[:], xb[:], AF.Square),
            waits=[("xd", xd_c), ("v", pl.c(f"_sqfree{p}"))], inc="s")
        s_sq = pl.c("s")
        pl.emit("vector", lambda e, b=b, sq=sqfp: e.reduce_sum(
            stats[:, b + 1:b + 2], sq[:], axis=AXX),
            waits=[("s", s_sq)], inc="v")
        pl.cnt[f"_sqfree{p}"] = pl.c("v")
        v_red = pl.c("v")
        pl.emit("scalar", lambda e, b=b: e.mul(stats[:, b + 2:b + 3],
                                               stats[:, b:b + 1], 1.0 / E),
                waits=[("v", v_red), ("g", pl.c(f"_lnG{p}")),
                       ("v", pl.c(f"_lnTS{p}"))], inc="s")
        pl.emit("scalar", lambda e, b=b: e.mul(stats[:, b + 3:b + 4],
                                               stats[:, b + 1:b + 2], 1.0 / E),
                inc="s")
        pl.cnt[f"_lnmul{p}"] = pl.c("s")
        s_m = pl.c("s")
        pl.emit("gpsimd", lambda e, b=b: e.tensor_tensor(
            stats[:, b + 4:b + 5], stats[:, b + 2:b + 3], stats[:, b + 2:b + 3],
            op=ALU.mult),
            waits=[("s", s_m)], inc="g")
        pl.emit("gpsimd", lambda e, b=b: e.tensor_tensor(
            stats[:, b + 4:b + 5], stats[:, b + 3:b + 4], stats[:, b + 4:b + 5],
            op=ALU.subtract), inc="g")
        pl.cnt[f"_lnG{p}"] = pl.c("g")
        g_v = pl.c("g")
        pl.emit("scalar", lambda e, b=b: e.activation(
            stats[:, b + 5:b + 6], stats[:, b + 4:b + 5], AF.Ln, bias=epsap[:]),
            waits=[("g", g_v), ("v", v_consts)], inc="s")
        pl.emit("scalar", lambda e, b=b: e.activation(
            stats[:, b + 6:b + 7], stats[:, b + 5:b + 6], AF.Exp, scale=-0.5),
            inc="s")
        s_r = pl.c("s")
        lnb = _sl(lnx_bf, lboff, [[lnx_bf.ap().ap[0][0], 128], [1, 1024]])
        pl.emit("vector", lambda e, xb=xbuf, lnb=lnb, b=b: e.tensor_scalar(
            lnb, xb[:], stats[:, b + 2:b + 3], stats[:, b + 6:b + 7],
            ALU.subtract, ALU.mult),
            waits=[("s", s_r), ("lt", pl.c(f"_lnbfree{p}"))], inc="v")
        pl.cnt[f"_lnTS{p}"] = pl.c("v")
        lnv[c] = pl.c("v")
        v_n = pl.c("v")
        outap = _sl(lnxT, c * 128, [[lnxT.ap().ap[0][0], 128], [S, 8], [1, 128]])
        pl.emit("sync", lambda e, o=outap, i=lnb: e.dma_start_transpose(out=o, in_=i),
                waits=[("v", v_n)], inc="lt", inc_n=16)
        pl.cnt[f"_lnbfree{p}"] = pl.c("lt")
        pl.cnt[f"_ltc{c}"] = pl.c("lt")
    lt_done = pl.c("lt")
    lt_chunk = [pl.c(f"_ltc{c}") for c in range(NB_T)]

    def rope_piece(dst_t, dst_off, src_off, W, tabcol, s_evac):
        sst = kstg.ap().ap[0][0]
        sap = bass.AP(kstg.ap().tensor, kstg.ap().offset + src_off,
                      [[sst, 128], [1, W]])
        dap = _sl(dst_t, dst_off, [[dst_t.ap().ap[0][0], 128], [1, W]])
        taps = [bass.AP(t.tensor, t.offset + tabcol, [[t.ap[0][0], 128], [1, W]])
                for t in (d_chi, d_clo, d_shi, d_slo)]
        w_tab = [("v", pl.c("_tabv")), ("g", pl.c("_tabg"))]
        for tb, ta in zip((tabhi, tablo, tabshi, tabslo), taps):
            pl.emit("sync", lambda e, tb=tb, ta=ta, W=W: e.dma_start(
                out=tb[:, 0:W], in_=ta), waits=w_tab, inc="xd", inc_n=16)
            w_tab = []
        xd_t = pl.c("xd")
        w_swp = [("v", pl.c("_swpv")), ("g", pl.c("_swpg")), ("lt", lt_done),
                 ("s", s_evac)]
        for (a, b) in ((0, 32), (32, 0), (64, 96), (96, 64)):
            pl.emit("sync", lambda e, a=a, b=b, W=W, so=src_off: e.dma_start(
                out=kswp[a:a + 32, 0:W],
                in_=bass.AP(kstg.ap().tensor, kstg.ap().offset + so,
                            [[sst, 128], [1, W]])[b:b + 32, :]),
                waits=w_swp, inc="xd", inc_n=16)
            w_swp = []
        xd_sw = pl.c("xd")
        pl.emit("vector", lambda e, s=sap, W=W: e.tensor_tensor(
            ropeA[:, 0:W], s, tabhi[:, 0:W], op=ALU.mult),
            waits=[("xd", xd_t), ("s", s_evac)], inc="v")
        pl.emit("vector", lambda e, W=W: e.tensor_tensor(
            corrA[:, 0:W], kswp[:, 0:W], tabshi[:, 0:W], op=ALU.mult),
            waits=[("xd", xd_sw)], inc="v")
        pl.emit("vector", lambda e, W=W: e.tensor_tensor(
            ropeA[:, 0:W], ropeA[:, 0:W], corrA[:, 0:W], op=ALU.add), inc="v")
        pl.emit("vector", lambda e, s=sap, W=W: e.tensor_tensor(
            corrA[:, 0:W], s, tablo[:, 0:W], op=ALU.mult), inc="v")
        pl.emit("vector", lambda e, W=W: e.tensor_tensor(
            ropeA[:, 0:W], ropeA[:, 0:W], corrA[:, 0:W], op=ALU.add), inc="v")
        pl.emit("vector", lambda e, W=W: e.tensor_tensor(
            corrA[:, 0:W], kswp[:, 0:W], tabslo[:, 0:W], op=ALU.mult), inc="v")
        pl.cnt["_swpv"] = pl.c("v")
        pl.emit("vector", lambda e, d=dap, W=W: e.tensor_tensor(
            d, ropeA[:, 0:W], corrA[:, 0:W], op=ALU.add), inc="v")
        pl.cnt["_tabv"] = pl.c("v")
        pl.cnt["_corrAv"] = pl.c("v")
        pl.cnt["_tabg"] = pl.c("g")
        pl.cnt["_swpg"] = pl.c("g")
        return pl.c("v")

    def norms_acc(col, first, dbg=None):
        w_extra = ([("out", pl.c("_nsdump"))] if DEBUG else [])
        pl.emit("tensor", lambda e: e.matmul(
            ps[6][0:2, :], onesblk[:, 0:2], sqb[:],
            start=True, stop=True),
            waits=[("v", pl.c("_nrmfree")), ("s", pl.c("_sqbv")),
                   ("xd", pl.c("_nrmxd")), ("v", v_consts),
                   ("s", pl.c("_rbread"))] + w_extra, inc="t")
        t_n = pl.c("t")
        if DEBUG and dbg is not None:
            pl.emit("vector", lambda e, s=dbg: e.reduce_max(
                dbgnrm[0:2, s:s + 1], ps[6][0:2, :], axis=AXX),
                waits=[("t", t_n)], inc="v")
        if first:
            pl.emit("vector", lambda e, col=col: e.reduce_max(
                nrm_sc[0:2, col:col + 1], ps[6][0:2, :], axis=AXX),
                waits=[("t", t_n)], inc="v")
        else:
            pl.emit("vector", lambda e, col=col: e.reduce_max(
                nrm_sc[0:2, 1:2], ps[6][0:2, :], axis=AXX),
                waits=[("t", t_n)], inc="v")
            pl.emit("vector", lambda e, col=col: e.tensor_tensor(
                nrm_sc[0:2, col:col + 1], nrm_sc[0:2, col:col + 1],
                nrm_sc[0:2, 1:2], op=ALU.max),
                waits=[("v", pl.c("v"))], inc="v")
        pl.cnt["_nrmfree"] = pl.c("v")
        if DEBUG and dbg is not None:
            pl.emit("sync", lambda e, s=dbg: e.dma_start(
                out=bass.AP(d_dbg_ns.tensor, d_dbg_ns.offset + s * 16,
                            [[d_dbg_ns.ap[0][0], 128], [1, 16]]),
                in_=nrm_sc[:]),
                waits=[("v", pl.c("v"))], inc="out", inc_n=16)
            pl.cnt["_nsdump"] = pl.c("out")

    def proj_stream(wbf, bias_t, bias_col, ntok, dstT, dst_base, norm_col,
                    dbg_seg=False):
        for seg in range(ntok // 512):
            par = seg % 2
            pb = ps[par]
            tok = seg * 512
            lt_w = lt_done
            for kc in range(8):
                w0 = ([("lt", lt_w), ("s", pl.c("_wcast")),
                       ("s", pl.c(f"_pev{par}"))] if kc == 0 else [])
                pl.emit("tensor", lambda e, pb=pb, kc=kc, tok=tok, wbf=wbf: e.matmul(
                    pb[:], wbf[:, kc * 128:(kc + 1) * 128],
                    _sl(lnxT, kc * S + tok, [[lnxT.ap().ap[0][0], 128], [1, 512]]),
                    start=(kc == 0), stop=(kc == 7)), waits=w0, inc="t")
            t_k = pl.c("t")
            pl.emit("scalar", lambda e, pb=pb, par=par, b=bias_col, bt=bias_t:
                    e.activation(kstg[:, par * 512:(par + 1) * 512], pb[:],
                                 AF.Identity, bias=bt[:, b:b + 1]),
                    waits=[("t", t_k), ("xd", pl.c(f"_kstgx{par}")),
                           ("v", pl.c(f"_kstgv{par}")),
                           ("g", pl.c(f"_kstgg{par}"))], inc="s")
            pl.cnt[f"_pev{par}"] = pl.c("s")
            w_sq = ([("out", pl.c("_sqbdump"))] if DEBUG else [])
            pl.emit("scalar", lambda e, par=par: e.activation(
                sqb[:], kstg[:, par * 512:(par + 1) * 512], AF.Square),
                waits=[("v", pl.c("_nrmfree"))] + w_sq, inc="s")
            pl.cnt["_sqbv"] = pl.c("s")
            if DEBUG and dbg_seg:
                pl.emit("sync", lambda e, s=seg: e.dma_start(
                    out=bass.AP(d_dbg_sqb.tensor, d_dbg_sqb.offset + s * 512,
                                [[d_dbg_sqb.ap[0][0], 128], [1, 512]]),
                    in_=sqb[:]),
                    waits=[("s", pl.c("_sqbv"))], inc="out", inc_n=16)
                pl.cnt["_sqbdump"] = pl.c("out")
            norms_acc(norm_col, first=(seg == 0),
                      dbg=(seg if dbg_seg else None))
            rv = rope_piece(dstT, dst_base + tok, par * 512, 512, tok_base + tok,
                            pl.c(f"_pev{par}"))
            pl.cnt[f"_kstgx{par}"] = pl.c("xd")
            pl.cnt[f"_kstgv{par}"] = rv
            pl.cnt[f"_kstgg{par}"] = pl.c("g")

    qT_v = {}
    kT_v = {}
    v_vpad_g = {}
    s_biasH_g = {}
    for g in range(2):
        pl.emit("sync", lambda e, g=g: e.dma_start(
            out=bvb_s[:], in_=bass.AP(d_bvb.tensor, d_bvb.offset + 512 * g,
                                      [[d_bvb.ap[0][0], 128], [1, 512]])),
            waits=[("v", v_vpad_g.get(g - 1, 0))],
            inc="xd", inc_n=16)
        xd_bvb = pl.c("xd")

        for cc in range(4):
            for kc in range(8):
                for (dsrc, dst, base) in ((d_wkv, wbfa, 512 * g + 128 * cc),
                                          (d_wq, wbfb, 512 * g + 128 * cc)):
                    pl.emit("sync", lambda e, kc=kc, dd=dsrc, b=base: e.dma_start(
                        out=wst[:, 0:128],
                        in_=dd[kc * 128:(kc + 1) * 128, b:b + 128]),
                        waits=[("s", pl.c("_wkfree"))],
                        inc="xd", inc_n=16)
                    xd_w = pl.c("xd")
                    pl.emit("scalar", lambda e, kc=kc, dst=dst: e.copy(
                        dst[:, kc * 128:(kc + 1) * 128], wst[:, 0:128]),
                        waits=[("xd", xd_w)], inc="s")
                    pl.cnt["_wkfree"] = pl.c("s")
            pl.cnt["_wcast"] = pl.c("s")

            tok_base = 0
            proj_stream(wbfa, bkv_s, 4 * g + cc, S, kT, cc * S, 8 + 2 * cc,
                        dbg_seg=(g == 0 and cc == 0))
            kT_v[(g, cc)] = pl.c("v")
            proj_stream(wbfb, bq_s, 4 * g + cc, 2048, qT, cc * 2048, 2 * cc)
            qT_v[(g, cc)] = pl.c("v")

        v_nrm = pl.c("v")
        for cc in range(4):
            pl.emit("sync", lambda e, cc=cc: e.dma_start(
                out=brow[0:1, 2 * cc:2 * cc + 2],
                in_=nrm_sc[0:2, 2 * cc:2 * cc + 1]),
                waits=[("v", v_nrm), ("g", pl.c("_browg"))], inc="xd", inc_n=16)
            pl.emit("sync", lambda e, cc=cc: e.dma_start(
                out=brow[0:1, 8 + 2 * cc:8 + 2 * cc + 2],
                in_=nrm_sc[0:2, 8 + 2 * cc:8 + 2 * cc + 1]),
                inc="xd", inc_n=16)
        xd_sp = pl.c("xd")
        pl.cnt["_nrmxd"] = xd_sp
        pl.emit("gpsimd", lambda e: e.tensor_tensor(
            brow[0:1, 16:24], brow[0:1, 0:8], brow[0:1, 8:16], op=ALU.mult),
            waits=[("xd", xd_sp)], inc="g")
        g_b = pl.c("g")
        pl.emit("scalar", lambda e: e.activation(brow[0:1, 16:24],
                                                 brow[0:1, 16:24], AF.Sqrt),
                waits=[("g", g_b)], inc="s")
        s_b = pl.c("s")
        pl.emit("gpsimd", lambda e: e.tensor_scalar(
            brow[0:1, 16:24], brow[0:1, 16:24], -0.13125, -2.0, ALU.mult, ALU.add),
            waits=[("s", s_b)], inc="g")
        g_b2 = pl.c("g")
        pl.cnt["_browg"] = g_b2
        pl.emit("tensor", lambda e: e.matmul(
            ps[7][0:64, 0:8], ones_f[0:1, 0:64], brow[0:1, 16:24],
            start=True, stop=True, tile_position=(0, 0)),
            waits=[("g", g_b2), ("v", v_consts), ("s", pl.c("_p7free"))], inc="t")
        pl.emit("tensor", lambda e: e.matmul(
            ps[7][64:128, 0:8], ones_f[0:1, 0:64], brow[0:1, 16:24],
            start=True, stop=True, tile_position=(0, 64)), inc="t")
        t_b = pl.c("t")
        pl.emit("scalar", lambda e, g=g: e.copy(biasH[:, 8 * g:8 * g + 8],
                                                ps[7][0:128, 0:8]),
                waits=[("t", t_b)], inc="s")
        s_biasH_g[g] = pl.c("s")
        pl.cnt["_p7free"] = pl.c("s")
        if DEBUG and g == 0:
            pl.emit("sync", lambda e: e.dma_start(out=d_dbg_bh0, in_=biasH[:]),
                    waits=[("s", s_biasH_g[0])], inc="out", inc_n=16)
            pl.emit("sync", lambda e: e.dma_start(out=d_dbg_nrm, in_=nrm_sc[:]),
                    inc="out", inc_n=16)
            pl.emit("sync", lambda e: e.dma_start(out=d_dbg_brow, in_=brow[:]),
                    inc="out", inc_n=16)
            pl.emit("sync", lambda e: e.dma_start(out=d_dbg_ps6, in_=dbgnrm[:]),
                    waits=[("v", pl.c("v"))], inc="out", inc_n=16)

        for hh in range(8):
            pl.emit("vector", lambda e, hh=hh: e.memset(
                _sl(vpad, hh * 65 + 64,
                    [[vpad.ap().ap[0][0], 128], [520, NB_T], [1, 1]]), 1.0),
                waits=[("t", pl.c("_vpadt"))], inc="v")
        for vh in range(2):
            for kc in range(8):
                for qq in range(2):
                    pl.emit("sync", lambda e, kc=kc, g=g, vh=vh, qq=qq: e.dma_start(
                        out=wst[:],
                        in_=d_wkv[kc * 128:(kc + 1) * 128,
                                  E + 512 * g + 256 * vh + 128 * qq:
                                  E + 512 * g + 256 * vh + 128 * (qq + 1)]),
                        waits=[("s", pl.c("_wvfree")),
                               ("t", pl.c("_wvbft"))],
                        inc="xd", inc_n=16)
                    xd_w = pl.c("xd")
                    pl.emit("scalar", lambda e, kc=kc, qq=qq: e.copy(
                        wvbf[:, kc * 256 + 128 * qq:kc * 256 + 128 * (qq + 1)],
                        wst[:]),
                        waits=[("xd", xd_w)], inc="s")
                    pl.cnt["_wvfree"] = pl.c("s")
            s_wv = pl.c("s")
            for tc in range(NB_T):
                pb = ps[2 + tc % 2]
                for kc in range(8):
                    w0 = ([("s", s_wv), ("xd", xd_bvb), ("lt", lt_done),
                           ("v", pl.c(f"_vev{tc % 2}"))] if kc == 0 else [])
                    pl.emit("tensor", lambda e, pb=pb, kc=kc, tc=tc: e.matmul(
                        bass.AP(pb.ap().tensor, pb.ap().offset,
                                [[pb.ap().ap[0][0], 128], [1, 256]]),
                        _sl(lnxT, kc * S + tc * 128,
                            [[lnxT.ap().ap[0][0], 128], [1, 128]]),
                        wvbf[:, kc * 256:(kc + 1) * 256],
                        start=(kc == 0), stop=(kc == 7)), waits=w0, inc="t")
                t_v = pl.c("t")
                outap = _sl(vpad, tc * 520 + 4 * vh * 65,
                            [[vpad.ap().ap[0][0], 128], [65, 4], [1, 64]])
                pl.emit("vector", lambda e, pb=pb, o=outap, vh=vh: e.tensor_tensor(
                    o, bass.AP(pb.ap().tensor, pb.ap().offset,
                               [[pb.ap().ap[0][0], 128], [1, 256]]),
                    bvb_s[:, 256 * vh:256 * (vh + 1)], op=ALU.add),
                    waits=[("t", t_v)], inc="v")
                pl.cnt[f"_vev{tc % 2}"] = pl.c("v")
            pl.cnt["_wvbft"] = pl.c("t")
        v_vpad = pl.c("v")
        v_vpad_g[g] = v_vpad

        wg_sched = None
        if g == 1:
            t_lnxT = pl.c("t")

            def wg_emitA(ti):
                w = [("v", pl.c("_qRed")), ("s", pl.c("_qA"))]
                if ti == 0:
                    w.append(("t", t_lnxT))
                pl.emit("sync", lambda e, ti=ti: e.dma_start(
                    out=ft2[:, 0:1024], in_=d_wg[ti * 128:(ti + 1) * 128, :]),
                    waits=w, inc="xd", inc_n=16)
                xd_w = pl.c("xd")
                emit_quant(ft2, 1024, fqb, ft1, fc1, fc2, fc3, fmb, fsc, frc,
                           wait_src=[("xd", xd_w)], part="A")
                pl.cnt["_wgxd"] = xd_w

            def wg_emitB(ti):
                emit_quant(ft2, 1024, fqb, ft1, fc1, fc2, fc3, fmb, fsc, frc,
                           wait_src=[("xd", pl.c("_wgxd"))], part="B")
                v_q = pl.c("v")
                outap = _sl(wgqT, ti * 128,
                            [[wgqT.ap().ap[0][0], 128], [I, 8], [1, 128]])
                inap = bass.AP(fqb.ap().tensor, fqb.ap().offset,
                               [[fqb.ap().ap[0][0], 128], [1, 1024]])
                pl.emit("sync", lambda e, o=outap, i=inap: e.dma_start_transpose(
                    out=o, in_=i), waits=[("v", v_q)], inc="tp", inc_n=16)
                pl.cnt["_fqbxd"] = pl.c("tp")

            wg_sched = {"a": 0, "b": 0, "n": 22}

        def wg_splice(frac):
            if wg_sched is None:
                return
            target = min(22, int(frac * 22 + 1e-9) + 1)
            while wg_sched["a"] < target:
                if wg_sched["a"] > 0:
                    wg_emitB(wg_sched["b"])
                    wg_sched["b"] += 1
                wg_emitA(wg_sched["a"])
                wg_sched["a"] += 1

        def emit_renorm(st):
            t_pv, pc, j = st
            for hi, pv in enumerate((ps[4], ps[5])):
                pl.emit("scalar", lambda e, pv=pv: e.activation(
                    rrow[64:65, 0:512], pv[64:65, :], AF.Ln),
                    waits=[("t", t_pv), ("t", pl.c("_rrowfree"))], inc="s")
                pl.emit("scalar", lambda e: e.activation(
                    rrow[64:65, 0:512], rrow[64:65, 0:512], AF.Exp,
                    scale=-1.0), inc="s")
                s_rc = pl.c("s")
                pl.emit("tensor", lambda e: e.matmul(
                    ps[6][0:64, :], ones_f[64:65, 0:64], rrow[64:65, 0:512],
                    start=True, stop=True),
                    waits=[("s", s_rc), ("s", pl.c("_rbpsfree")),
                           ("v", pl.c("_nrmfree"))], inc="t")
                t_rb = pl.c("t")
                pl.cnt["_rrowfree"] = t_rb
                pl.emit("scalar", lambda e: e.copy(rb_sb[0:64, :], ps[6][0:64, :]),
                        waits=[("t", t_rb), ("v", pl.c("_rbsbv"))], inc="s")
                s_rb = pl.c("s")
                pl.cnt["_rbpsfree"] = s_rb
                pl.cnt["_rbread"] = s_rb
                oap = _sl(attnoutT, pc * 2048 + j * 512,
                          [[attnoutT.ap().ap[0][0], 128], [1, 512]])
                if hi == 0:
                    pl.emit("vector", lambda e, pv=pv, o=oap: e.tensor_tensor(
                        bass.AP(o.tensor, o.offset, o.ap)[0:64, :],
                        pv[0:64, :], rb_sb[0:64, :], op=ALU.mult),
                        waits=[("s", s_rb)], inc="v")
                else:
                    pl.emit("vector", lambda e, pv=pv: e.tensor_tensor(
                        tmpB[0:64, :], pv[0:64, :], rb_sb[0:64, :], op=ALU.mult),
                        waits=[("s", s_rb), ("gd", pl.c("_tmpBfree"))], inc="v")
                    v_tb = pl.c("v")
                    pl.emit("sync", lambda e, o=oap: e.dma_start(
                        out=bass.AP(o.tensor, o.offset, o.ap)[64:128, :],
                        in_=tmpB[0:64, :]),
                        waits=[("v", v_tb)], inc="gd", inc_n=16)
                    pl.cnt["_tmpBfree"] = pl.c("gd")
                pl.cnt["_rbsbv"] = pl.c("v")
            pl.cnt["_pvfree"] = pl.c("v")

        pending = None
        for cc in range(4):
            hA, hB = 8 * g + 2 * cc, 8 * g + 2 * cc + 1
            pc = 4 * g + cc
            for j in range(4):
                nkb = 8 * j + 8
                base_wait = [("v", max(qT_v[(g, cc)], kT_v[(g, cc)], v_vpad)),
                             ("s", s_biasH_g[g])]
                deps = {}
                kperms = {}
                for kb in range(nkb):
                    own = kb < 4 * j + 4
                    kperm = kb if own else 16 + (kb - (4 * j + 4))
                    rel = (kperm - 4 * j) if own else (kperm - 16 - 4 * j)
                    kperms[kb] = kperm
                    sa = ps[kb % 2]
                    sb = ps[2 + kb % 2]
                    w0 = [("s", pl.c(f"_exA{kb % 2}"))] + (base_wait if kb == 0 else [])
                    pl.emit("tensor", lambda e, sa=sa, cc=cc, kp=kperm, j=j: e.matmul(
                        sa[:], _sl(kT, cc * S + kp * 128,
                                   [[kT.ap().ap[0][0], 128], [1, 128]])[0:64, :],
                        _sl(qT, cc * 2048 + j * 512,
                            [[qT.ap().ap[0][0], 128], [1, 512]])[0:64, :],
                        start=True, stop=True, tile_position=(0, 0)),
                        waits=w0, inc="t")
                    t_sa = pl.c("t")
                    w1 = [("s", pl.c(f"_exB{kb % 2}"))] + (base_wait if kb == 0 else [])
                    pl.emit("tensor", lambda e, sb=sb, cc=cc, kp=kperm, j=j: e.matmul(
                        sb[:], _sl(kT, cc * S + kp * 128,
                                   [[kT.ap().ap[0][0], 128], [1, 128]])[64:128, :],
                        _sl(qT, cc * 2048 + j * 512,
                            [[qT.ap().ap[0][0], 128], [1, 512]])[64:128, :],
                        start=True, stop=True, tile_position=(64, 0)),
                        waits=w1, inc="t")
                    t_sb = pl.c("t")
                    pl.emit("scalar", lambda e, sa=sa, kb=kb, hA=hA: e.activation(
                        prA[:, (kb % 2) * 512:(kb % 2 + 1) * 512], sa[:], AF.Exp,
                        bias=biasH[:, hA:hA + 1], scale=0.125),
                        waits=[("t", t_sa), ("t", pl.c(f"_pvA{kb % 2}"))], inc="s")
                    pl.cnt[f"_exA{kb % 2}"] = pl.c("s")
                    pl.emit("scalar", lambda e, sb=sb, kb=kb, hB=hB: e.activation(
                        prB[:, (kb % 2) * 512:(kb % 2 + 1) * 512], sb[:], AF.Exp,
                        bias=biasH[:, hB:hB + 1], scale=0.125),
                        waits=[("t", t_sb), ("t", pl.c(f"_pvB{kb % 2}"))], inc="s")
                    pl.cnt[f"_exB{kb % 2}"] = pl.c("s")
                    s_ex = pl.c("s")
                    dep = ("s", s_ex)
                    nv = False
                    for ci in range(4):
                        mop = None
                        if rel == ci:
                            mop = tri if own else pm
                        elif 0 <= ci < rel:
                            mop = "zero"
                        if mop is None:
                            continue
                        for pr in (prA, prB):
                            col = (kb % 2) * 512 + ci * 128
                            if mop == "zero":
                                pl.emit("vector", lambda e, pr=pr, col=col: e.memset(
                                    pr[:, col:col + 128], 0.0),
                                    waits=[("s", s_ex)], inc="v")
                            else:
                                pl.emit("vector", lambda e, pr=pr, col=col, m=mop:
                                        e.tensor_tensor(
                                    pr[:, col:col + 128], pr[:, col:col + 128],
                                    m[:], op=ALU.mult),
                                    waits=[("s", s_ex), ("xd", xd_consts)], inc="v")
                        nv = True
                    if nv:
                        dep = ("v", pl.c("v"))
                    deps[kb] = dep


                    if kb >= 1:
                        pkb = kb - 1
                        extra = [("v", pl.c("_pvfree"))] if pkb == 0 else []
                        pl.emit("tensor", lambda e, cc=cc, kp=kperms[pkb], kb=pkb,
                                nkb=nkb: e.matmul(
                            ps[4][0:65, :],
                            _sl(vpad, kp * 520 + 2 * cc * 65,
                                [[vpad.ap().ap[0][0], 128], [1, 65]]),
                            prA[:, (kb % 2) * 512:(kb % 2 + 1) * 512],
                            start=(kb == 0), stop=(kb == nkb - 1)),
                            waits=[deps[pkb]] + extra, inc="t")
                        pl.cnt[f"_pvA{pkb % 2}"] = pl.c("t")
                        pl.emit("tensor", lambda e, cc=cc, kp=kperms[pkb], kb=pkb,
                                nkb=nkb: e.matmul(
                            ps[5][0:65, :],
                            _sl(vpad, kp * 520 + (2 * cc + 1) * 65,
                                [[vpad.ap().ap[0][0], 128], [1, 65]]),
                            prB[:, (kb % 2) * 512:(kb % 2 + 1) * 512],
                            start=(kb == 0), stop=(kb == nkb - 1)),
                            waits=[deps[pkb]], inc="t")
                        pl.cnt[f"_pvB{pkb % 2}"] = pl.c("t")
                pkb = nkb - 1
                pl.emit("tensor", lambda e, cc=cc, kp=kperms[pkb], kb=pkb,
                        nkb=nkb: e.matmul(
                    ps[4][0:65, :],
                    _sl(vpad, kp * 520 + 2 * cc * 65,
                        [[vpad.ap().ap[0][0], 128], [1, 65]]),
                    prA[:, (kb % 2) * 512:(kb % 2 + 1) * 512],
                    start=(kb == 0), stop=(kb == nkb - 1)),
                    waits=[deps[pkb]], inc="t")
                pl.cnt[f"_pvA{pkb % 2}"] = pl.c("t")
                pl.emit("tensor", lambda e, cc=cc, kp=kperms[pkb], kb=pkb,
                        nkb=nkb: e.matmul(
                    ps[5][0:65, :],
                    _sl(vpad, kp * 520 + (2 * cc + 1) * 65,
                        [[vpad.ap().ap[0][0], 128], [1, 65]]),
                    prB[:, (kb % 2) * 512:(kb % 2 + 1) * 512],
                    start=(kb == 0), stop=(kb == nkb - 1)),
                    waits=[deps[pkb]], inc="t")
                pl.cnt[f"_pvB{pkb % 2}"] = pl.c("t")
                emit_renorm((pl.c("t"), pc, j))
                wg_splice((4 * cc + j + 1) / 16.0)
        pl.cnt["_vpadt"] = pl.c("t")
        while wg_sched is not None and wg_sched["b"] < wg_sched["a"]:
            wg_emitB(wg_sched["b"])
            wg_sched["b"] += 1

    if DEBUG:
        pl.emit("sync", lambda e: e.dma_start(out=d_dbg_lnxT, in_=lnxT[:]),
                waits=[("lt", lt_done)], inc="out", inc_n=16)
        pl.emit("sync", lambda e: e.dma_start(out=d_dbg_qT, in_=qT[:]),
                waits=[("v", pl.c("v"))], inc="out", inc_n=16)
        pl.emit("sync", lambda e: e.dma_start(out=d_dbg_kT, in_=kT[:]),
                waits=[("v", pl.c("v"))], inc="out", inc_n=16)
        pl.emit("sync", lambda e: e.dma_start(out=d_dbg_vp, in_=vpad[:]),
                waits=[("v", pl.c("v"))], inc="out", inc_n=16)
        pl.emit("sync", lambda e: e.dma_start(out=d_dbg_bh, in_=biasH[:]),
                waits=[("s", pl.c("s"))], inc="out", inc_n=16)
        pl.emit("sync", lambda e: e.dma_start(out=d_dbg_ao, in_=attnoutT[:]),
                waits=[("v", pl.c("v")), ("xd", pl.c("xd"))],
                inc="out", inc_n=16)
        out_extra = 16 * 34
    else:
        out_extra = 0

    xd_attn = pl.c("xd")
    gd_attn = pl.c("gd")
    v_attn = pl.c("v")
    t_attn = pl.c("t")

    pl.emit("sync", lambda e: e.dma_start(out=bob_s[:], in_=d_bob),
            waits=[("v", v_attn), ("t", t_attn), ("s", pl.c("s"))],
            inc="xd", inc_n=16)
    xd_bob = pl.c("xd")
    for kc in range(8):
        for half in range(2):
            pl.emit("sync", lambda e, kc=kc, h=half: e.dma_start(
                out=ft1[:, 0:512], in_=d_wo[kc * 128:(kc + 1) * 128,
                                            h * 512:(h + 1) * 512]),
                waits=[("s", pl.c("_wofree")), ("v", v_attn), ("t", t_attn)],
                inc="xd", inc_n=16)
            xd_w = pl.c("xd")
            pl.emit("scalar", lambda e, kc=kc, h=half: e.copy(
                wo_bf[:, kc * 1024 + h * 512:kc * 1024 + (h + 1) * 512],
                ft1[:, 0:512]),
                waits=[("xd", xd_w)], inc="s")
            pl.cnt["_wofree"] = pl.c("s")
    s_wo = pl.c("s")

    def quant_weight(dram, nrow_tiles, W, dstT, dst_nchunks, dst_chunk_w):
        for ti in range(nrow_tiles):
            pl.emit("sync", lambda e, ti=ti, W=W, dd=dram: e.dma_start(
                out=ft2[:, 0:W], in_=dd[ti * 128:(ti + 1) * 128, :]),
                waits=[("v", pl.c("v")), ("s", max(pl.c("s"), s_wo)),
                       ("xd", xd_attn), ("t", t_attn)],
                inc="xd", inc_n=16)
            xd_w = pl.c("xd")
            emit_quant(ft2, W, fqb, ft1, fc1, fc2, fc3, fmb, fsc, frc,
                       wait_src=[("xd", xd_w)])
            v_q = pl.c("v")
            outap = _sl(dstT, ti * 128,
                        [[dstT.ap().ap[0][0], 128], [dst_chunk_w, dst_nchunks],
                         [1, 128]])
            inap = bass.AP(fqb.ap().tensor, fqb.ap().offset,
                           [[fqb.ap().ap[0][0], 128], [1, W]])
            pl.emit("sync", lambda e, o=outap, i=inap: e.dma_start_transpose(
                out=o, in_=i), waits=[("v", v_q)], inc="tp", inc_n=16)
            pl.cnt["_fqbxd"] = pl.c("tp")

    if DEBUG:
        pl.emit("sync", lambda e: e.dma_start(out=d_dbg_wgq, in_=wgqT[:]),
                waits=[("tp", pl.c("tp"))], inc="out", inc_n=16)
    quant_weight(d_wu, 22, 1024, wuqT, 8, I)
    if DEBUG:
        pl.emit("sync", lambda e: e.dma_start(out=d_dbg_wuq, in_=wuqT[:]),
                waits=[("tp", pl.c("tp"))], inc="out", inc_n=16)
    for ti in range(8):
        for (i0, iw) in ((0, 1024), (1024, 1024), (2048, 768)):
            pl.emit("sync", lambda e, ti=ti, i0=i0, iw=iw: e.dma_start(
                out=ft2[:, 0:iw],
                in_=d_wd[ti * 128:(ti + 1) * 128, i0:i0 + iw]),
                waits=[("v", pl.c("v")), ("s", pl.c("s")), ("xd", xd_attn),
                       ("t", t_attn)],
                inc="xd", inc_n=16)
            xd_w = pl.c("xd")
            emit_quant(ft2, iw, fqb, ft1, fc1, fc2, fc3, fmb, fsc, frc,
                       wait_src=[("xd", xd_w)])
            v_q = pl.c("v")
            outap = _sl(wdqT, ti * 128 + (i0 // 128) * 1024,
                        [[wdqT.ap().ap[0][0], 128], [1024, iw // 128], [1, 128]])
            inap = bass.AP(fqb.ap().tensor, fqb.ap().offset,
                           [[fqb.ap().ap[0][0], 128], [1, iw]])
            pl.emit("sync", lambda e, o=outap, i=inap, iw=iw: e.dma_start_transpose(
                out=o, in_=i), waits=[("v", v_q)], inc="tp", inc_n=16)
            pl.cnt["_fqbxd"] = pl.c("tp")
    if DEBUG:
        pl.emit("sync", lambda e: e.dma_start(out=d_dbg_wdq, in_=wdqT[:]),
                waits=[("tp", pl.c("tp"))], inc="out", inc_n=16)
    xd_wq = pl.c("xd")
    tp_wq = pl.c("tp")

    out_n = 0
    pl.cnt["_zfree"] = v_attn
    pl.cnt["_gufree"] = pl.c("v")
    pl.cnt["_yfree"] = v_attn
    for tc in range(NQB):
        pl.emit("sync", lambda e, tc=tc: e.dma_start(
            out=xres[:], in_=d_x[tc * 128:(tc + 1) * 128, :]),
            waits=[("v", pl.c("_xrfree")), ("xd", xd_wq)], inc="xd", inc_n=16)
        xd_xr = pl.c("xd")
        for half in range(2):
            pb = ps[half]
            for kc in range(8):
                w0 = ([("s", s_wo), ("v", pl.c("_zfree")), ("xd", xd_attn),
                       ("gd", gd_attn),
                       ("s", pl.c("_exA0")), ("s", pl.c("_exA1")),
                       ("s", pl.c("_exB0")), ("s", pl.c("_exB1"))]
                      if kc == 0 else [])
                pl.emit("tensor", lambda e, pb=pb, kc=kc, tc=tc, h=half: e.matmul(
                    pb[:], _sl(attnoutT, kc * 2048 + tc * 128,
                               [[attnoutT.ap().ap[0][0], 128], [1, 128]]),
                    wo_bf[:, kc * 1024 + h * 512:kc * 1024 + (h + 1) * 512],
                    start=(kc == 0), stop=(kc == 7)), waits=w0, inc="t")
            t_z = pl.c("t")
            pl.emit("vector", lambda e, pb=pb, h=half: e.tensor_tensor(
                x2[:, h * 512:(h + 1) * 512], pb[:],
                xres[:, h * 512:(h + 1) * 512], op=ALU.add),
                waits=[("t", t_z), ("xd", xd_xr), ("v", pl.c("_x2free"))], inc="v")
        pl.emit("vector", lambda e: e.tensor_tensor(x2[:], x2[:], bob_s[:],
                                                    op=ALU.add),
                waits=[("xd", xd_bob)], inc="v")
        pl.cnt["_zfree"] = pl.c("v")
        pl.cnt["_xrfree"] = pl.c("v")
        v_x2 = pl.c("v")
        if DEBUG and tc == 0:
            pl.emit("sync", lambda e: e.dma_start(out=d_dbg_x2, in_=x2[:]),
                    waits=[("v", v_x2)], inc="out", inc_n=16)
        pl.emit("vector", lambda e: e.reduce_sum(fstats[:, 0:1], x2[:], axis=AXX),
                waits=[("v", v_x2), ("s", pl.c("_fst23"))], inc="v")
        pl.emit("scalar", lambda e: e.activation(ft1[:, 0:1024], x2[:], AF.Square),
                waits=[("v", v_x2), ("v", pl.c("v"))], inc="s")
        s_sq = pl.c("s")
        pl.emit("vector", lambda e: e.reduce_sum(fstats[:, 1:2], ft1[:, 0:1024],
                                                 axis=AXX),
                waits=[("s", s_sq)], inc="v")
        v_red = pl.c("v")
        pl.emit("scalar", lambda e: e.mul(fstats[:, 2:3], fstats[:, 0:1], 1.0 / E),
                waits=[("v", v_red)], inc="s")
        pl.emit("scalar", lambda e: e.mul(fstats[:, 3:4], fstats[:, 1:2], 1.0 / E),
                inc="s")
        pl.cnt["_fst23"] = pl.c("s")
        s_m = pl.c("s")
        pl.emit("gpsimd", lambda e: e.tensor_tensor(
            fstats[:, 4:5], fstats[:, 2:3], fstats[:, 2:3], op=ALU.mult),
            waits=[("s", s_m)], inc="g")
        pl.emit("gpsimd", lambda e: e.tensor_tensor(
            fstats[:, 4:5], fstats[:, 3:4], fstats[:, 4:5], op=ALU.subtract),
            inc="g")
        g_v = pl.c("g")
        pl.emit("scalar", lambda e: e.activation(fstats[:, 5:6], fstats[:, 4:5],
                                                 AF.Ln, bias=epsap[:]),
                waits=[("g", g_v)], inc="s")
        pl.emit("scalar", lambda e: e.activation(fstats[:, 6:7], fstats[:, 5:6],
                                                 AF.Exp, scale=-0.5), inc="s")
        s_r = pl.c("s")
        pl.emit("vector", lambda e: e.tensor_scalar(
            ft2[:, 0:1024], x2[:], fstats[:, 2:3], fstats[:, 6:7],
            ALU.subtract, ALU.mult),
            waits=[("s", s_r), ("out", pl.c("_ft2out"))], inc="v")
        emit_quant(ft2, 1024, fqb, ft1, fc1, fc2, fc3, fmb, fsc, frc)
        v_xq = pl.c("v")
        if DEBUG and tc == 0:
            pl.emit("sync", lambda e: e.dma_start(out=d_dbg_ln2, in_=ft2[:, 0:1024]),
                    waits=[("v", v_xq)], inc="out", inc_n=16)
            pl.emit("sync", lambda e: e.dma_start(out=d_dbg_xq, in_=fqb[:, 0:1024]),
                    waits=[("v", v_xq)], inc="out", inc_n=16)
        outap = bass.AP(xqT.ap().tensor, xqT.ap().offset,
                        [[xqT.ap().ap[0][0], 128], [128, 8], [1, 128]])
        inap = bass.AP(fqb.ap().tensor, fqb.ap().offset,
                       [[fqb.ap().ap[0][0], 128], [1, 1024]])
        pl.emit("sync", lambda e, o=outap, i=inap: e.dma_start_transpose(
            out=o, in_=i), waits=[("v", v_xq), ("t", pl.c("_xqTt"))],
            inc="tp", inc_n=16)
        pl.cnt["_fqbxd"] = pl.c("tp")
        tp_xq = pl.c("tp")
        if DEBUG and tc == 0:
            pl.emit("sync", lambda e: e.dma_start(out=d_dbg_xqT, in_=xqT[:]),
                    waits=[("tp", tp_xq)], inc="out", inc_n=16)
        pieces = [(0, 1024), (1024, 1024), (2048, 768)]
        for (i0, iw) in pieces:
            ntiles = (iw + 511) // 512
            for (wsrc, pbase) in ((wgqT, 2), (wuqT, 4)):
                for nt in range(ntiles):
                    nw = min(512, iw - nt * 512)
                    pdst = ps[pbase + nt]
                    for kc in range(8):
                        w0 = ([("xd", xd_wq), ("tp", max(tp_xq, tp_wq)),
                               ("v", pl.c("_gufree"))]
                              if kc == 0 else [])
                        pl.emit("tensor", lambda e, pdst=pdst, kc=kc, i0=i0, nt=nt,
                                nw=nw, wsrc=wsrc: e.matmul(
                            bass.AP(pdst.ap().tensor, pdst.ap().offset,
                                    [[pdst.ap().ap[0][0], 128], [1, nw]]),
                            xqT[:, kc * 128:(kc + 1) * 128],
                            _sl(wsrc, kc * I + i0 + nt * 512,
                                [[wsrc.ap().ap[0][0], 128], [1, nw]]),
                            start=(kc == 0), stop=(kc == 7)), waits=w0, inc="t")
            t_gu = pl.c("t")
            for nt in range(ntiles):
                nw = min(512, iw - nt * 512)
                pl.emit("scalar", lambda e, nt=nt, nw=nw: e.activation(
                    ft1[:, nt * 512:nt * 512 + nw],
                    bass.AP(ps[2 + nt].ap().tensor, ps[2 + nt].ap().offset,
                            [[ps[2 + nt].ap().ap[0][0], 128], [1, nw]]), AF.Silu),
                    waits=[("t", t_gu), ("v", pl.c("v"))], inc="s")
                s_sil = pl.c("s")
                pl.emit("vector", lambda e, nt=nt, nw=nw: e.tensor_tensor(
                    ft2[:, nt * 512:nt * 512 + nw],
                    ft1[:, nt * 512:nt * 512 + nw],
                    bass.AP(ps[4 + nt].ap().tensor, ps[4 + nt].ap().offset,
                            [[ps[4 + nt].ap().ap[0][0], 128], [1, nw]]),
                    op=ALU.mult),
                    waits=[("s", s_sil)], inc="v")
            pl.cnt["_gufree"] = pl.c("v")
            emit_quant(ft2, iw, fqb, ft1, fc1, fc2, fc3, fmb, fsc, frc)
            v_hq = pl.c("v")
            nch = iw // 128
            outap = _sl(hqT, (i0 // 128) * 128,
                        [[hqT.ap().ap[0][0], 128], [128, nch], [1, 128]])
            inap = bass.AP(fqb.ap().tensor, fqb.ap().offset,
                           [[fqb.ap().ap[0][0], 128], [1, iw]])
            pl.emit("sync", lambda e, o=outap, i=inap: e.dma_start_transpose(
                out=o, in_=i), waits=[("v", v_hq), ("t", pl.c("_hqTt"))],
                inc="tp", inc_n=16)
            pl.cnt["_fqbxd"] = pl.c("tp")
        tp_hq = pl.c("tp")
        if DEBUG and tc == 0:
            pl.emit("sync", lambda e: e.dma_start(out=d_dbg_hqT, in_=hqT[:]),
                    waits=[("tp", tp_hq)], inc="out", inc_n=16)
        for half in range(2):
            yb = ps[6 + half]
            for kc in range(22):
                w0 = ([("tp", tp_hq), ("v", pl.c("_yfree")),
                       ("s", pl.c("_rbpsfree"))] if kc == 0 else [])
                pl.emit("tensor", lambda e, yb=yb, kc=kc, h=half: e.matmul(
                    yb[:], hqT[:, kc * 128:(kc + 1) * 128],
                    _sl(wdqT, kc * 1024 + h * 512,
                        [[wdqT.ap().ap[0][0], 128], [1, 512]]),
                    start=(kc == 0), stop=(kc == 21)), waits=w0, inc="t")
            t_y = pl.c("t")
            if half == 1:
                pl.cnt["_hqTt"] = t_y
                pl.cnt["_xqTt"] = t_y
            pl.emit("vector", lambda e, yb=yb, h=half: e.tensor_tensor(
                ft2[:, h * 512:(h + 1) * 512], yb[:],
                x2[:, h * 512:(h + 1) * 512], op=ALU.add),
                waits=[("t", t_y)], inc="v")
        pl.cnt["_yfree"] = pl.c("v")
        pl.cnt["_x2free"] = pl.c("v")
        v_out = pl.c("v")
        pl.emit("sync", lambda e, tc=tc: e.dma_start(
            out=d_out[tc * 128:(tc + 1) * 128, :], in_=ft2[:, 0:1024]),
            waits=[("v", v_out)], inc="out", inc_n=16)
        out_n += 16
        pl.cnt["_ft2out"] = pl.c("out")

    pl.emit("gpsimd", lambda e: e.engine_nop(), waits=[("out", out_n + out_extra)])

    import contextlib
    with contextlib.ExitStack() as stk:
        sems = {s: stk.enter_context(nc.semaphore(s)) for s in SEMS}
        block = stk.enter_context(nc.Block())

        def mk(engname):
            def run(eng):
                for fn, waits, inc, inc_n in pl.items[engname]:
                    for s, v in waits:
                        eng.wait_ge(sems[s], v)
                    inst = fn(eng)
                    if inc is not None:
                        inst.then_inc(sems[inc], inc_n)
            return run

        block.sync(mk("sync"))
        block.tensor(mk("tensor"))
        block.vector(mk("vector"))
        block.scalar(mk("scalar"))
        block.gpsimd(mk("gpsimd"))
    return nc


_NC_CACHE = {}


def _host_constants():
    inv = 1.0 / (10000.0 ** (np.arange(0, DH, 2, dtype=np.float64) / DH))
    pos = np.arange(S, dtype=np.float64)
    ang = pos[None, :] * inv[:, None]
    cos, sin = np.cos(ang), np.sin(ang)
    chi = cos.astype(np.float32)
    clo = (cos - chi).astype(np.float32)
    shi = sin.astype(np.float32)
    slo = (sin - shi).astype(np.float32)
    CHI = np.tile(chi, (4, 1)).astype(np.float32)
    CLO = np.tile(clo, (4, 1)).astype(np.float32)
    SHI = np.concatenate([-shi, shi, -shi, shi], 0).astype(np.float32)
    SLO = np.concatenate([-slo, slo, -slo, slo], 0).astype(np.float32)
    rk = np.arange(128)[:, None]
    rq = np.arange(128)[None, :]
    TRI = (rk <= rq).astype(ml_dtypes.bfloat16)
    return CHI, CLO, SHI, SLO, TRI


def kernel(L0_attn_x, L0_attn_Wq, L0_attn_bq, L0_attn_Wkv, L0_attn_bkv,
           L0_attn_Wo, L0_attn_bo, L0_ffn_w_gate, L0_ffn_w_up, L0_ffn_w_down):
    x = np.asarray(L0_attn_x, dtype=np.float32)
    wq = np.ascontiguousarray(np.asarray(L0_attn_Wq, dtype=np.float32))
    bq = np.asarray(L0_attn_bq, dtype=np.float32)
    wkv = np.ascontiguousarray(np.asarray(L0_attn_Wkv, dtype=np.float32))
    bkv = np.asarray(L0_attn_bkv, dtype=np.float32)
    wo = np.ascontiguousarray(np.asarray(L0_attn_Wo, dtype=np.float32))
    bo = np.asarray(L0_attn_bo, dtype=np.float32)
    wg = np.ascontiguousarray(np.asarray(L0_ffn_w_gate, dtype=np.float32))
    wu = np.ascontiguousarray(np.asarray(L0_ffn_w_up, dtype=np.float32))
    wd = np.ascontiguousarray(np.asarray(L0_ffn_w_down, dtype=np.float32))

    CHI, CLO, SHI, SLO, TRI = _host_constants()
    bq_s = np.ascontiguousarray(bq.reshape(8, 128).T)
    bkv_s = np.ascontiguousarray(bkv.reshape(16, 128).T)
    bvb = np.ascontiguousarray(np.broadcast_to(bkv[E:2 * E][None, :], (128, E)))
    bob = np.ascontiguousarray(np.broadcast_to(bo[None, :], (128, E)))

    in_maps = []
    for c in range(8):
        b, p = c // 2, c % 2
        blocks = [2 * i + p for i in range(16)] + \
                 [2 * i + (1 - p) for i in range(16)]
        colperm = np.concatenate(
            [np.arange(g * 128, (g + 1) * 128) for g in blocks])
        xb = x[b].reshape(32, 128, E)[blocks].reshape(S, E)
        pmv = np.full((128, 128), 1.0 if p == 1 else 0.0, ml_dtypes.bfloat16)
        in_maps.append({
            "x": np.ascontiguousarray(xb),
            "wq": wq, "wkv": wkv, "wo": wo, "wg": wg, "wu": wu, "wd": wd,
            "bq": bq_s, "bkv": bkv_s, "bvb": bvb, "bob": bob,
            "chi": np.ascontiguousarray(CHI[:, colperm]),
            "clo": np.ascontiguousarray(CLO[:, colperm]),
            "shi": np.ascontiguousarray(SHI[:, colperm]),
            "slo": np.ascontiguousarray(SLO[:, colperm]),
            "tri": TRI, "pm": pmv,
        })

    if "nc" not in _NC_CACHE:
        _NC_CACHE["nc"] = build_core()
    nc = _NC_CACHE["nc"]
    res = run_bass_kernel_spmd(nc, in_maps, core_ids=list(range(8)))

    out = np.empty((B, S, E), dtype=np.float32)
    for c in range(8):
        b, p = c // 2, c % 2
        co = np.asarray(res.results[c]["out"], dtype=np.float32).reshape(16, 128, E)
        out.reshape(B, 32, 128, E)[b, p::2] = co
    if DEBUG:
        globals()["last_debug"] = res.results
    return out


# version 12 (final)
# speedup vs baseline: 1.0517x; 1.0517x over previous
import numpy as np
import ml_dtypes
import concourse.bass as bass
import concourse.mybir as mybir
from concourse.bass_utils import run_bass_kernel_spmd

F32 = mybir.dt.float32
BF16 = mybir.dt.bfloat16
AF = mybir.ActivationFunctionType
ALU = mybir.AluOpType
AXX = mybir.AxisListType.X

B, S, E = 4, 4096, 1024
DEBUG = False
H, DH = 16, 64
I = 2816
NB_T = 32
NQB = 16
EPS = 1e-5


class Plan:
    ENGS = ("sync", "tensor", "vector", "scalar", "gpsimd")

    def __init__(self):
        self.items = {e: [] for e in self.ENGS}
        self.cnt = {}

    def c(self, sem):
        return self.cnt.get(sem, 0)

    def emit(self, eng, fn, waits=(), inc=None, inc_n=1):
        if inc is not None:
            self.cnt[inc] = self.cnt.get(inc, 0) + inc_n
        ws = {}
        for s, v in waits:
            if v > 0:
                ws[s] = max(ws.get(s, 0), v)
        self.items[eng].append((fn, list(ws.items()), inc, inc_n))


def _sl(t, off, dims):
    a = t.ap()
    return bass.AP(a.tensor, a.offset + off, dims)


def build_core():
    nc = bass.Bass("TRN2", target_bir_lowering=False, dynamic_dma_scratch_size=2048)

    d_x = nc.dram_tensor("x", [S, E], F32, kind="ExternalInput").ap()
    d_wq = nc.dram_tensor("wq", [E, E], F32, kind="ExternalInput").ap()
    d_wkv = nc.dram_tensor("wkv", [E, 2 * E], F32, kind="ExternalInput").ap()
    d_wo = nc.dram_tensor("wo", [E, E], F32, kind="ExternalInput").ap()
    d_wg = nc.dram_tensor("wg", [I, E], F32, kind="ExternalInput").ap()
    d_wu = nc.dram_tensor("wu", [I, E], F32, kind="ExternalInput").ap()
    d_wd = nc.dram_tensor("wd", [E, I], F32, kind="ExternalInput").ap()
    d_bq = nc.dram_tensor("bq", [128, 8], F32, kind="ExternalInput").ap()
    d_bkv = nc.dram_tensor("bkv", [128, 16], F32, kind="ExternalInput").ap()
    d_bvb = nc.dram_tensor("bvb", [128, E], F32, kind="ExternalInput").ap()
    d_bob = nc.dram_tensor("bob", [128, E], F32, kind="ExternalInput").ap()
    d_chi = nc.dram_tensor("chi", [128, S], F32, kind="ExternalInput").ap()
    d_clo = nc.dram_tensor("clo", [128, S], F32, kind="ExternalInput").ap()
    d_shi = nc.dram_tensor("shi", [128, S], F32, kind="ExternalInput").ap()
    d_slo = nc.dram_tensor("slo", [128, S], F32, kind="ExternalInput").ap()
    d_tri = nc.dram_tensor("tri", [128, 128], BF16, kind="ExternalInput").ap()
    d_pm = nc.dram_tensor("pm", [128, 128], BF16, kind="ExternalInput").ap()
    d_out = nc.dram_tensor("out", [2048, E], F32, kind="ExternalOutput").ap()
    if DEBUG:
        d_dbg_lnxT = nc.dram_tensor("dbg_lnxT", [128, 8 * S], BF16,
                                    kind="ExternalOutput").ap()
        d_dbg_qT = nc.dram_tensor("dbg_qT", [128, 4 * 2048], BF16,
                                  kind="ExternalOutput").ap()
        d_dbg_kT = nc.dram_tensor("dbg_kT", [128, 4 * S], BF16,
                                  kind="ExternalOutput").ap()
        d_dbg_vp = nc.dram_tensor("dbg_vp", [128, NB_T * 520], BF16,
                                  kind="ExternalOutput").ap()
        d_dbg_bh = nc.dram_tensor("dbg_bh", [128, 16], F32,
                                  kind="ExternalOutput").ap()
        d_dbg_ao = nc.dram_tensor("dbg_ao", [128, 16384], BF16,
                                  kind="ExternalOutput").ap()
        d_dbg_x2 = nc.dram_tensor("dbg_x2", [128, 1024], F32,
                                  kind="ExternalOutput").ap()
        d_dbg_xq = nc.dram_tensor("dbg_xq", [128, 1024], BF16,
                                  kind="ExternalOutput").ap()
        d_dbg_hqT = nc.dram_tensor("dbg_hqT", [128, 2816], BF16,
                                   kind="ExternalOutput").ap()
        d_dbg_wgq = nc.dram_tensor("dbg_wgq", [128, 8 * I], BF16,
                                   kind="ExternalOutput").ap()
        d_dbg_wuq = nc.dram_tensor("dbg_wuq", [128, 8 * I], BF16,
                                   kind="ExternalOutput").ap()
        d_dbg_wdq = nc.dram_tensor("dbg_wdq", [128, 22 * 1024], BF16,
                                   kind="ExternalOutput").ap()
        d_dbg_ln2 = nc.dram_tensor("dbg_ln2", [128, 1024], F32,
                                   kind="ExternalOutput").ap()
        d_dbg_xqT = nc.dram_tensor("dbg_xqT", [128, 1024], BF16,
                                   kind="ExternalOutput").ap()
        d_dbg_bh0 = nc.dram_tensor("dbg_bh0", [128, 16], F32,
                                   kind="ExternalOutput").ap()
        d_dbg_nrm = nc.dram_tensor("dbg_nrm", [128, 16], F32,
                                   kind="ExternalOutput").ap()
        d_dbg_brow = nc.dram_tensor("dbg_brow", [128, 24], F32,
                                    kind="ExternalOutput").ap()
        d_dbg_ns = nc.dram_tensor("dbg_ns", [128, 128], F32,
                                  kind="ExternalOutput").ap()
        d_dbg_sqb = nc.dram_tensor("dbg_sqb", [128, 4096], BF16,
                                   kind="ExternalOutput").ap()
        d_dbg_ps6 = nc.dram_tensor("dbg_ps6", [128, 16], F32,
                                   kind="ExternalOutput").ap()

    SBUF_TOT = nc.sbuf_top
    off = [((nc.sbuf_base + 31) // 32) * 32]

    def alloc(name, shape, dt, at=None):
        nbytes = shape[1] * (2 if dt == BF16 else 4)
        if at is None:
            at = off[0]
            off[0] = at + ((nbytes + 31) // 32) * 32
            assert off[0] <= SBUF_TOT, f"SBUF overflow at {name}: {off[0]}"
        else:
            assert at + nbytes <= SBUF_TOT, f"SBUF overflow at {name}"
        return nc.alloc_sbuf_tensor_at(name, shape, dt, offset=at)

    tri = alloc("tri", [128, 128], BF16)
    pm = alloc("pm", [128, 128], BF16)
    ones_f = alloc("ones_f", [128, 64], F32)
    onesblk = alloc("onesblk", [128, 2], BF16)
    epsap = alloc("epsap", [128, 1], F32)
    bq_s = alloc("bq_s", [128, 8], F32)
    bkv_s = alloc("bkv_s", [128, 16], F32)
    biasH = alloc("biasH", [128, 16], F32)
    nrm_sc = alloc("nrm_sc", [128, 16], F32)
    brow = alloc("brow", [128, 24], F32)
    fmb = alloc("fmb", [128, 64], F32)
    fsc = alloc("fsc", [128, 64], F32)
    frc = alloc("frc", [128, 64], F32)
    dbgnrm = alloc("dbgnrm", [128, 16], F32)
    attnoutT = alloc("attnoutT", [128, 16384], BF16)
    base_after_const = off[0]

    lnxT = alloc("lnxT", [128, 8 * S], BF16)
    kT = alloc("kT", [128, 4 * S], BF16)
    qT = alloc("qT", [128, 4 * 2048], BF16)
    vpad = alloc("vpad", [128, NB_T * 520], BF16)
    y_base = off[0]
    prA = alloc("prA", [128, 1024], BF16)
    prB = alloc("prB", [128, 1024], BF16)
    rb_sb = alloc("rb_sb", [128, 512], F32)
    tmpB = alloc("tmpB", [128, 512], BF16)
    rrow = alloc("rrow", [128, 512], F32)
    corrA = alloc("corrA", [128, 512], F32)
    corrB = alloc("corrB", [128, 512], F32)
    kswp = alloc("kswp", [128, 512], F32)
    ropeA = alloc("ropeA", [128, 512], F32)
    tabhi = alloc("tabhi", [128, 512], F32)
    tablo = alloc("tablo", [128, 512], F32)
    tabshi = alloc("tabshi", [128, 512], F32)
    tabslo = alloc("tabslo", [128, 512], F32)
    kstg = alloc("kstg", [128, 1024], F32)
    sqb = alloc("sqb", [128, 512], BF16)
    wst = alloc("wst", [128, 128], F32)
    wbfa = alloc("wbfa", [128, 1024], BF16)
    wbfb = alloc("wbfb", [128, 1024], BF16)
    wvbf = alloc("wvbf", [128, 2048], BF16)
    bvb_s = alloc("bvb_s", [128, 512], F32)
    assert off[0] <= SBUF_TOT, f"phase2 SBUF over: {off[0]}"
    xo = [y_base]

    def xalloc(name, shape, dt):
        nbytes = shape[1] * (2 if dt == BF16 else 4)
        at = xo[0]
        xo[0] = at + ((nbytes + 31) // 32) * 32
        assert xo[0] <= off[0], f"X overlay over Y at {name}"
        return nc.alloc_sbuf_tensor_at(name, shape, dt, offset=at)

    lnx_bf = xalloc("lnx_bf", [128, 2048], BF16)
    xs0 = xalloc("xs0", [128, 1024], F32)
    xs1 = xalloc("xs1", [128, 1024], F32)
    sqf = xalloc("sqf", [128, 1024], F32)
    sqf2 = xalloc("sqf2", [128, 1024], F32)
    stats = xalloc("stats", [128, 16], F32)

    fo = [base_after_const]

    def falloc(name, shape, dt):
        nbytes = shape[1] * (2 if dt == BF16 else 4)
        at = fo[0]
        fo[0] = at + ((nbytes + 31) // 32) * 32
        assert fo[0] <= SBUF_TOT, f"FFN SBUF over at {name}: {fo[0]}"
        return nc.alloc_sbuf_tensor_at(name, shape, dt, offset=at)

    wgqT = falloc("wgqT", [128, 8 * I], BF16)
    wuqT = falloc("wuqT", [128, 8 * I], BF16)
    wdqT = falloc("wdqT", [128, 22 * 1024], BF16)
    wo_bf = falloc("wo_bf", [128, 8 * 1024], BF16)
    bob_s = falloc("bob_s", [128, E], F32)
    ft2b = nc.alloc_sbuf_tensor_at("ft2b", [128, 1024], F32, offset=fo[0])
    xres = falloc("xres", [128, 1024], F32)
    x2 = falloc("x2", [128, 1024], F32)
    ft1 = falloc("ft1", [128, 1024], F32)
    ft2 = falloc("ft2", [128, 1024], F32)
    fc1 = falloc("fc1", [128, 1024], BF16)
    fc2 = falloc("fc2", [128, 1024], BF16)
    fc3 = falloc("fc3", [128, 1024], BF16)
    fqb = falloc("fqb", [128, 1024], BF16)
    hqT = falloc("hqT", [128, 2816], BF16)
    xqT = falloc("xqT", [128, 1024], BF16)
    sgb = falloc("sgb", [128, 1024], BF16)
    fstats = nrm_sc
    assert fo[0] <= SBUF_TOT, f"FFN SBUF over: {fo[0]}"

    ps = [nc.alloc_psum_tensor(f"p{i}", [128, 512], F32) for i in range(8)]

    pl = Plan()
    SEMS = ["xd", "gd", "lt", "tp", "v", "s", "t", "g", "out"]

    def emit_quant(src_t, W, dst_t, ay, c1, c2, c3, mb, sc, rc, wait_src=()):
        nb = W // 16
        nbp = max(nb, 64)
        assert src_t.ap().ap[-1][1] * src_t.ap().ap[-2][1] >= nbp * 16 or True
        a = src_t.ap()
        pst = a.ap[0][0]
        src = bass.AP(a.tensor, a.offset, [[pst, 128], [1, W]])
        src3 = bass.AP(a.tensor, a.offset, [[pst, 128], [16, nbp], [1, 16]])
        d = dst_t.ap()
        dst = bass.AP(d.tensor, d.offset, [[d.ap[0][0], 128], [1, W]])
        mbv = _sl(mb, 0, [[mb.ap().ap[0][0], 128], [1, nbp]])
        scv = _sl(sc, 0, [[sc.ap().ap[0][0], 128], [1, nbp]])
        rcv = _sl(rc, 0, [[rc.ap().ap[0][0], 128], [1, nbp]])
        rbb = _sl(rc, 0, [[rc.ap().ap[0][0], 128], [1, nb], [0, 16]])
        scb = _sl(sc, 0, [[sc.ap().ap[0][0], 128], [1, nb], [0, 16]])
        ayp = _sl(ay, 0, [[ay.ap().ap[0][0], 128], [1, W]])
        c1p = _sl(c1, 0, [[c1.ap().ap[0][0], 128], [1, W]])
        c2p = _sl(c2, 0, [[c2.ap().ap[0][0], 128], [1, W]])
        c3p = _sl(c3, 0, [[c3.ap().ap[0][0], 128], [1, W]])

        sgp = _sl(sgb, 0, [[sgb.ap().ap[0][0], 128], [1, W]])
        pl.emit("scalar", lambda e: e.activation(ayp, src, AF.Abs),
                waits=list(wait_src) + [("v", pl.c("v"))], inc="s")
        pl.emit("scalar", lambda e: e.activation(sgp, src, AF.Sign),
                waits=[("v", pl.c("_sgbfree"))], inc="s")
        sA = pl.c("s")
        pl.emit("vector", lambda e: e.tensor_reduce(
            mbv, src3, axis=AXX, op=ALU.max, apply_absolute_value=True),
            waits=wait_src, inc="v")
        pl.emit("vector", lambda e: e.tensor_scalar(scv, mbv, 1.0 / 6.0, 1e-12,
                                                    ALU.mult, ALU.max),
                waits=[("v", pl.c("v"))], inc="v")
        pl.emit("vector", lambda e: e.reciprocal(rcv, scv),
                waits=[("v", pl.c("v"))], inc="v")
        pl.emit("vector", lambda e: e.tensor_tensor(ayp, ayp, rbb, op=ALU.mult),
                waits=[("s", sA), ("v", pl.c("v"))])
        pl.emit("vector", lambda e: e.tensor_scalar(c1p, ayp, 0.25, None, ALU.is_gt))
        pl.emit("vector", lambda e: e.tensor_scalar(c2p, ayp, 0.75, None, ALU.is_gt))
        pl.emit("vector", lambda e: e.tensor_tensor(c1p, c1p, c2p, op=ALU.add))
        pl.emit("vector", lambda e: e.tensor_scalar(c2p, ayp, 1.25, None, ALU.is_gt))
        pl.emit("vector", lambda e: e.tensor_scalar(c3p, ayp, 1.75, None, ALU.is_gt))
        pl.emit("vector", lambda e: e.tensor_tensor(c2p, c2p, c3p, op=ALU.add))
        pl.emit("vector", lambda e: e.tensor_tensor(c1p, c1p, c2p, op=ALU.add))
        pl.emit("vector", lambda e: e.tensor_scalar(c2p, ayp, 2.5, None, ALU.is_gt))
        pl.emit("vector", lambda e: e.tensor_scalar(c3p, ayp, 3.5, None, ALU.is_gt))
        pl.emit("vector", lambda e: e.tensor_tensor(c2p, c2p, c3p, op=ALU.add))
        pl.emit("vector", lambda e: e.scalar_tensor_tensor(c2p, c1p, 0.5, c2p,
                                                           op0=ALU.mult, op1=ALU.add))
        pl.emit("vector", lambda e: e.tensor_scalar(c3p, ayp, 5.0, None, ALU.is_gt))
        pl.emit("vector", lambda e: e.scalar_tensor_tensor(c1p, c3p, 2.0, c2p,
                                                           op0=ALU.mult, op1=ALU.add))
        pl.emit("vector", lambda e: e.tensor_tensor(c1p, c1p, sgp, op=ALU.mult),
                waits=[("s", sA)], inc="v")
        pl.cnt["_sgbfree"] = pl.c("v")
        pl.emit("vector", lambda e: e.tensor_tensor(dst, c1p, scb, op=ALU.mult),
                waits=[("tp", pl.c("_fqbxd"))], inc="v")

    for t, dsrc in ((tri, d_tri), (pm, d_pm), (bq_s, d_bq), (bkv_s, d_bkv)):
        pl.emit("sync", lambda e, t=t, dd=dsrc: e.dma_start(out=t[:], in_=dd),
                inc="xd", inc_n=16)
    pl.emit("vector", lambda e: e.memset(ones_f[:], 1.0), inc="v")
    pl.emit("vector", lambda e: e.memset(epsap[:], EPS), inc="v")
    pl.emit("vector", lambda e: e.memset(onesblk[:], 0.0), inc="v")
    pl.emit("vector", lambda e: e.memset(onesblk[0:64, 0:1], 1.0), inc="v")
    pl.emit("vector", lambda e: e.memset(onesblk[64:128, 1:2], 1.0), inc="v")
    pl.emit("vector", lambda e: e.memset(nrm_sc[:], -777.0), inc="v")
    v_consts = pl.c("v")
    xd_consts = pl.c("xd")

    lnv = {}
    for c in range(NB_T):
        p = c % 2
        b = 8 * p
        xbuf = xs0 if p == 0 else xs1
        sqfp = sqf if p == 0 else sqf2
        lboff = p * 1024
        pl.emit("sync", lambda e, c=c, xb=xbuf: e.dma_start(
            out=xb[:], in_=d_x[c * 128:(c + 1) * 128, :]),
            waits=[("v", lnv.get(c - 2, 0))], inc="xd", inc_n=16)
        xd_c = pl.c("xd")
        pl.emit("vector", lambda e, xb=xbuf, b=b: e.reduce_sum(
            stats[:, b:b + 1], xb[:], axis=AXX),
            waits=[("xd", xd_c), ("s", pl.c(f"_lnmul{p}"))], inc="v")
        pl.emit("scalar", lambda e, xb=xbuf, sq=sqfp: e.activation(
            sq[:], xb[:], AF.Square),
            waits=[("xd", xd_c), ("v", pl.c(f"_sqfree{p}"))], inc="s")
        s_sq = pl.c("s")
        pl.emit("vector", lambda e, b=b, sq=sqfp: e.reduce_sum(
            stats[:, b + 1:b + 2], sq[:], axis=AXX),
            waits=[("s", s_sq)], inc="v")
        pl.cnt[f"_sqfree{p}"] = pl.c("v")
        v_red = pl.c("v")
        pl.emit("scalar", lambda e, b=b: e.mul(stats[:, b + 2:b + 3],
                                               stats[:, b:b + 1], 1.0 / E),
                waits=[("v", v_red), ("g", pl.c(f"_lnG{p}")),
                       ("v", pl.c(f"_lnTS{p}"))], inc="s")
        pl.emit("scalar", lambda e, b=b: e.mul(stats[:, b + 3:b + 4],
                                               stats[:, b + 1:b + 2], 1.0 / E),
                inc="s")
        pl.cnt[f"_lnmul{p}"] = pl.c("s")
        s_m = pl.c("s")
        pl.emit("gpsimd", lambda e, b=b: e.tensor_tensor(
            stats[:, b + 4:b + 5], stats[:, b + 2:b + 3], stats[:, b + 2:b + 3],
            op=ALU.mult),
            waits=[("s", s_m)], inc="g")
        pl.emit("gpsimd", lambda e, b=b: e.tensor_tensor(
            stats[:, b + 4:b + 5], stats[:, b + 3:b + 4], stats[:, b + 4:b + 5],
            op=ALU.subtract), inc="g")
        pl.cnt[f"_lnG{p}"] = pl.c("g")
        g_v = pl.c("g")
        pl.emit("scalar", lambda e, b=b: e.activation(
            stats[:, b + 5:b + 6], stats[:, b + 4:b + 5], AF.Ln, bias=epsap[:]),
            waits=[("g", g_v), ("v", v_consts)], inc="s")
        pl.emit("scalar", lambda e, b=b: e.activation(
            stats[:, b + 6:b + 7], stats[:, b + 5:b + 6], AF.Exp, scale=-0.5),
            inc="s")
        s_r = pl.c("s")
        lnb = _sl(lnx_bf, lboff, [[lnx_bf.ap().ap[0][0], 128], [1, 1024]])
        pl.emit("vector", lambda e, xb=xbuf, lnb=lnb, b=b: e.tensor_scalar(
            lnb, xb[:], stats[:, b + 2:b + 3], stats[:, b + 6:b + 7],
            ALU.subtract, ALU.mult),
            waits=[("s", s_r), ("lt", pl.c(f"_lnbfree{p}"))], inc="v")
        pl.cnt[f"_lnTS{p}"] = pl.c("v")
        lnv[c] = pl.c("v")
        v_n = pl.c("v")
        outap = _sl(lnxT, c * 128, [[lnxT.ap().ap[0][0], 128], [S, 8], [1, 128]])
        pl.emit("sync", lambda e, o=outap, i=lnb: e.dma_start_transpose(out=o, in_=i),
                waits=[("v", v_n)], inc="lt", inc_n=16)
        pl.cnt[f"_lnbfree{p}"] = pl.c("lt")
        pl.cnt[f"_ltc{c}"] = pl.c("lt")
    lt_done = pl.c("lt")
    lt_chunk = [pl.c(f"_ltc{c}") for c in range(NB_T)]

    def rope_piece(dst_t, dst_off, src_off, W, tabcol, s_evac):
        sst = kstg.ap().ap[0][0]
        sap = bass.AP(kstg.ap().tensor, kstg.ap().offset + src_off,
                      [[sst, 128], [1, W]])
        dap = _sl(dst_t, dst_off, [[dst_t.ap().ap[0][0], 128], [1, W]])
        taps = [bass.AP(t.tensor, t.offset + tabcol, [[t.ap[0][0], 128], [1, W]])
                for t in (d_chi, d_clo, d_shi, d_slo)]
        w_tab = [("v", pl.c("_tabv")), ("g", pl.c("_tabg"))]
        for tb, ta in zip((tabhi, tablo, tabshi, tabslo), taps):
            pl.emit("sync", lambda e, tb=tb, ta=ta, W=W: e.dma_start(
                out=tb[:, 0:W], in_=ta), waits=w_tab, inc="xd", inc_n=16)
            w_tab = []
        xd_t = pl.c("xd")
        w_swp = [("v", pl.c("_swpv")), ("g", pl.c("_swpg")), ("lt", lt_done),
                 ("s", s_evac)]
        for (a, b) in ((0, 32), (32, 0), (64, 96), (96, 64)):
            pl.emit("sync", lambda e, a=a, b=b, W=W, so=src_off: e.dma_start(
                out=kswp[a:a + 32, 0:W],
                in_=bass.AP(kstg.ap().tensor, kstg.ap().offset + so,
                            [[sst, 128], [1, W]])[b:b + 32, :]),
                waits=w_swp, inc="xd", inc_n=16)
            w_swp = []
        xd_sw = pl.c("xd")
        pl.emit("vector", lambda e, s=sap, W=W: e.tensor_tensor(
            ropeA[:, 0:W], s, tabhi[:, 0:W], op=ALU.mult),
            waits=[("xd", xd_t), ("s", s_evac)], inc="v")
        pl.emit("vector", lambda e, W=W: e.tensor_tensor(
            corrA[:, 0:W], kswp[:, 0:W], tabshi[:, 0:W], op=ALU.mult),
            waits=[("xd", xd_sw)], inc="v")
        pl.emit("vector", lambda e, W=W: e.tensor_tensor(
            ropeA[:, 0:W], ropeA[:, 0:W], corrA[:, 0:W], op=ALU.add), inc="v")
        pl.emit("vector", lambda e, s=sap, W=W: e.tensor_tensor(
            corrA[:, 0:W], s, tablo[:, 0:W], op=ALU.mult), inc="v")
        pl.emit("vector", lambda e, W=W: e.tensor_tensor(
            ropeA[:, 0:W], ropeA[:, 0:W], corrA[:, 0:W], op=ALU.add), inc="v")
        pl.emit("vector", lambda e, W=W: e.tensor_tensor(
            corrA[:, 0:W], kswp[:, 0:W], tabslo[:, 0:W], op=ALU.mult), inc="v")
        pl.cnt["_swpv"] = pl.c("v")
        pl.emit("vector", lambda e, d=dap, W=W: e.tensor_tensor(
            d, ropeA[:, 0:W], corrA[:, 0:W], op=ALU.add), inc="v")
        pl.cnt["_tabv"] = pl.c("v")
        pl.cnt["_corrAv"] = pl.c("v")
        pl.cnt["_tabg"] = pl.c("g")
        pl.cnt["_swpg"] = pl.c("g")
        return pl.c("v")

    def norms_acc(col, first, dbg=None):
        w_extra = ([("out", pl.c("_nsdump"))] if DEBUG else [])
        pl.emit("tensor", lambda e: e.matmul(
            ps[6][0:2, :], onesblk[:, 0:2], sqb[:],
            start=True, stop=True),
            waits=[("v", pl.c("_nrmfree")), ("s", pl.c("_sqbv")),
                   ("xd", pl.c("_nrmxd")), ("v", v_consts),
                   ("s", pl.c("_rbread"))] + w_extra, inc="t")
        t_n = pl.c("t")
        if DEBUG and dbg is not None:
            pl.emit("vector", lambda e, s=dbg: e.reduce_max(
                dbgnrm[0:2, s:s + 1], ps[6][0:2, :], axis=AXX),
                waits=[("t", t_n)], inc="v")
        if first:
            pl.emit("vector", lambda e, col=col: e.reduce_max(
                nrm_sc[0:2, col:col + 1], ps[6][0:2, :], axis=AXX),
                waits=[("t", t_n)], inc="v")
        else:
            pl.emit("vector", lambda e, col=col: e.reduce_max(
                nrm_sc[0:2, 1:2], ps[6][0:2, :], axis=AXX),
                waits=[("t", t_n)], inc="v")
            pl.emit("vector", lambda e, col=col: e.tensor_tensor(
                nrm_sc[0:2, col:col + 1], nrm_sc[0:2, col:col + 1],
                nrm_sc[0:2, 1:2], op=ALU.max),
                waits=[("v", pl.c("v"))], inc="v")
        pl.cnt["_nrmfree"] = pl.c("v")
        if DEBUG and dbg is not None:
            pl.emit("sync", lambda e, s=dbg: e.dma_start(
                out=bass.AP(d_dbg_ns.tensor, d_dbg_ns.offset + s * 16,
                            [[d_dbg_ns.ap[0][0], 128], [1, 16]]),
                in_=nrm_sc[:]),
                waits=[("v", pl.c("v"))], inc="out", inc_n=16)
            pl.cnt["_nsdump"] = pl.c("out")

    def proj_stream(wbf, bias_t, bias_col, ntok, dstT, dst_base, norm_col,
                    dbg_seg=False):
        for seg in range(ntok // 512):
            par = seg % 2
            pb = ps[par]
            tok = seg * 512
            lt_w = lt_done
            for kc in range(8):
                w0 = ([("lt", lt_w), ("s", pl.c("_wcast")),
                       ("s", pl.c(f"_pev{par}"))] if kc == 0 else [])
                pl.emit("tensor", lambda e, pb=pb, kc=kc, tok=tok, wbf=wbf: e.matmul(
                    pb[:], wbf[:, kc * 128:(kc + 1) * 128],
                    _sl(lnxT, kc * S + tok, [[lnxT.ap().ap[0][0], 128], [1, 512]]),
                    start=(kc == 0), stop=(kc == 7)), waits=w0, inc="t")
            t_k = pl.c("t")
            pl.emit("scalar", lambda e, pb=pb, par=par, b=bias_col, bt=bias_t:
                    e.activation(kstg[:, par * 512:(par + 1) * 512], pb[:],
                                 AF.Identity, bias=bt[:, b:b + 1]),
                    waits=[("t", t_k), ("xd", pl.c(f"_kstgx{par}")),
                           ("v", pl.c(f"_kstgv{par}")),
                           ("g", pl.c(f"_kstgg{par}"))], inc="s")
            pl.cnt[f"_pev{par}"] = pl.c("s")
            w_sq = ([("out", pl.c("_sqbdump"))] if DEBUG else [])
            pl.emit("scalar", lambda e, par=par: e.activation(
                sqb[:], kstg[:, par * 512:(par + 1) * 512], AF.Square),
                waits=[("v", pl.c("_nrmfree"))] + w_sq, inc="s")
            pl.cnt["_sqbv"] = pl.c("s")
            if DEBUG and dbg_seg:
                pl.emit("sync", lambda e, s=seg: e.dma_start(
                    out=bass.AP(d_dbg_sqb.tensor, d_dbg_sqb.offset + s * 512,
                                [[d_dbg_sqb.ap[0][0], 128], [1, 512]]),
                    in_=sqb[:]),
                    waits=[("s", pl.c("_sqbv"))], inc="out", inc_n=16)
                pl.cnt["_sqbdump"] = pl.c("out")
            norms_acc(norm_col, first=(seg == 0),
                      dbg=(seg if dbg_seg else None))
            rv = rope_piece(dstT, dst_base + tok, par * 512, 512, tok_base + tok,
                            pl.c(f"_pev{par}"))
            pl.cnt[f"_kstgx{par}"] = pl.c("xd")
            pl.cnt[f"_kstgv{par}"] = rv
            pl.cnt[f"_kstgg{par}"] = pl.c("g")

    qT_v = {}
    kT_v = {}
    v_vpad_g = {}
    s_biasH_g = {}
    for g in range(2):
        pl.emit("sync", lambda e, g=g: e.dma_start(
            out=bvb_s[:], in_=bass.AP(d_bvb.tensor, d_bvb.offset + 512 * g,
                                      [[d_bvb.ap[0][0], 128], [1, 512]])),
            waits=[("v", v_vpad_g.get(g - 1, 0))],
            inc="xd", inc_n=16)
        xd_bvb = pl.c("xd")

        for cc in range(4):
            for kc in range(8):
                for (dsrc, dst, base) in ((d_wkv, wbfa, 512 * g + 128 * cc),
                                          (d_wq, wbfb, 512 * g + 128 * cc)):
                    pl.emit("sync", lambda e, kc=kc, dd=dsrc, b=base: e.dma_start(
                        out=wst[:, 0:128],
                        in_=dd[kc * 128:(kc + 1) * 128, b:b + 128]),
                        waits=[("s", pl.c("_wkfree"))],
                        inc="xd", inc_n=16)
                    xd_w = pl.c("xd")
                    pl.emit("scalar", lambda e, kc=kc, dst=dst: e.copy(
                        dst[:, kc * 128:(kc + 1) * 128], wst[:, 0:128]),
                        waits=[("xd", xd_w)], inc="s")
                    pl.cnt["_wkfree"] = pl.c("s")
            pl.cnt["_wcast"] = pl.c("s")

            tok_base = 0
            proj_stream(wbfa, bkv_s, 4 * g + cc, S, kT, cc * S, 8 + 2 * cc,
                        dbg_seg=(g == 0 and cc == 0))
            kT_v[(g, cc)] = pl.c("v")
            proj_stream(wbfb, bq_s, 4 * g + cc, 2048, qT, cc * 2048, 2 * cc)
            qT_v[(g, cc)] = pl.c("v")

        v_nrm = pl.c("v")
        for cc in range(4):
            pl.emit("sync", lambda e, cc=cc: e.dma_start(
                out=brow[0:1, 2 * cc:2 * cc + 2],
                in_=nrm_sc[0:2, 2 * cc:2 * cc + 1]),
                waits=[("v", v_nrm), ("g", pl.c("_browg"))], inc="xd", inc_n=16)
            pl.emit("sync", lambda e, cc=cc: e.dma_start(
                out=brow[0:1, 8 + 2 * cc:8 + 2 * cc + 2],
                in_=nrm_sc[0:2, 8 + 2 * cc:8 + 2 * cc + 1]),
                inc="xd", inc_n=16)
        xd_sp = pl.c("xd")
        pl.cnt["_nrmxd"] = xd_sp
        pl.emit("gpsimd", lambda e: e.tensor_tensor(
            brow[0:1, 16:24], brow[0:1, 0:8], brow[0:1, 8:16], op=ALU.mult),
            waits=[("xd", xd_sp)], inc="g")
        g_b = pl.c("g")
        pl.emit("scalar", lambda e: e.activation(brow[0:1, 16:24],
                                                 brow[0:1, 16:24], AF.Sqrt),
                waits=[("g", g_b)], inc="s")
        s_b = pl.c("s")
        pl.emit("gpsimd", lambda e: e.tensor_scalar(
            brow[0:1, 16:24], brow[0:1, 16:24], -0.13125, -2.0, ALU.mult, ALU.add),
            waits=[("s", s_b)], inc="g")
        g_b2 = pl.c("g")
        pl.cnt["_browg"] = g_b2
        pl.emit("tensor", lambda e: e.matmul(
            ps[7][0:64, 0:8], ones_f[0:1, 0:64], brow[0:1, 16:24],
            start=True, stop=True, tile_position=(0, 0)),
            waits=[("g", g_b2), ("v", v_consts), ("s", pl.c("_p7free"))], inc="t")
        pl.emit("tensor", lambda e: e.matmul(
            ps[7][64:128, 0:8], ones_f[0:1, 0:64], brow[0:1, 16:24],
            start=True, stop=True, tile_position=(0, 64)), inc="t")
        t_b = pl.c("t")
        pl.emit("scalar", lambda e, g=g: e.copy(biasH[:, 8 * g:8 * g + 8],
                                                ps[7][0:128, 0:8]),
                waits=[("t", t_b)], inc="s")
        s_biasH_g[g] = pl.c("s")
        pl.cnt["_p7free"] = pl.c("s")
        if DEBUG and g == 0:
            pl.emit("sync", lambda e: e.dma_start(out=d_dbg_bh0, in_=biasH[:]),
                    waits=[("s", s_biasH_g[0])], inc="out", inc_n=16)
            pl.emit("sync", lambda e: e.dma_start(out=d_dbg_nrm, in_=nrm_sc[:]),
                    inc="out", inc_n=16)
            pl.emit("sync", lambda e: e.dma_start(out=d_dbg_brow, in_=brow[:]),
                    inc="out", inc_n=16)
            pl.emit("sync", lambda e: e.dma_start(out=d_dbg_ps6, in_=dbgnrm[:]),
                    waits=[("v", pl.c("v"))], inc="out", inc_n=16)

        for hh in range(8):
            pl.emit("vector", lambda e, hh=hh: e.memset(
                _sl(vpad, hh * 65 + 64,
                    [[vpad.ap().ap[0][0], 128], [520, NB_T], [1, 1]]), 1.0),
                waits=[("t", pl.c("_vpadt"))], inc="v")
        for vh in range(2):
            for kc in range(8):
                for qq in range(2):
                    pl.emit("sync", lambda e, kc=kc, g=g, vh=vh, qq=qq: e.dma_start(
                        out=wst[:],
                        in_=d_wkv[kc * 128:(kc + 1) * 128,
                                  E + 512 * g + 256 * vh + 128 * qq:
                                  E + 512 * g + 256 * vh + 128 * (qq + 1)]),
                        waits=[("s", pl.c("_wvfree")),
                               ("t", pl.c("_wvbft"))],
                        inc="xd", inc_n=16)
                    xd_w = pl.c("xd")
                    pl.emit("scalar", lambda e, kc=kc, qq=qq: e.copy(
                        wvbf[:, kc * 256 + 128 * qq:kc * 256 + 128 * (qq + 1)],
                        wst[:]),
                        waits=[("xd", xd_w)], inc="s")
                    pl.cnt["_wvfree"] = pl.c("s")
            s_wv = pl.c("s")
            for tc in range(NB_T):
                pb = ps[2 + tc % 2]
                for kc in range(8):
                    w0 = ([("s", s_wv), ("xd", xd_bvb), ("lt", lt_done),
                           ("v", pl.c(f"_vev{tc % 2}"))] if kc == 0 else [])
                    pl.emit("tensor", lambda e, pb=pb, kc=kc, tc=tc: e.matmul(
                        bass.AP(pb.ap().tensor, pb.ap().offset,
                                [[pb.ap().ap[0][0], 128], [1, 256]]),
                        _sl(lnxT, kc * S + tc * 128,
                            [[lnxT.ap().ap[0][0], 128], [1, 128]]),
                        wvbf[:, kc * 256:(kc + 1) * 256],
                        start=(kc == 0), stop=(kc == 7)), waits=w0, inc="t")
                t_v = pl.c("t")
                outap = _sl(vpad, tc * 520 + 4 * vh * 65,
                            [[vpad.ap().ap[0][0], 128], [65, 4], [1, 64]])
                pl.emit("vector", lambda e, pb=pb, o=outap, vh=vh: e.tensor_tensor(
                    o, bass.AP(pb.ap().tensor, pb.ap().offset,
                               [[pb.ap().ap[0][0], 128], [1, 256]]),
                    bvb_s[:, 256 * vh:256 * (vh + 1)], op=ALU.add),
                    waits=[("t", t_v)], inc="v")
                pl.cnt[f"_vev{tc % 2}"] = pl.c("v")
            pl.cnt["_wvbft"] = pl.c("t")
        v_vpad = pl.c("v")
        v_vpad_g[g] = v_vpad

        def emit_renorm(st):
            t_pv, pc, j = st
            for hi, pv in enumerate((ps[4], ps[5])):
                pl.emit("scalar", lambda e, pv=pv: e.activation(
                    rrow[64:65, 0:512], pv[64:65, :], AF.Ln),
                    waits=[("t", t_pv), ("t", pl.c("_rrowfree"))], inc="s")
                pl.emit("scalar", lambda e: e.activation(
                    rrow[64:65, 0:512], rrow[64:65, 0:512], AF.Exp,
                    scale=-1.0), inc="s")
                s_rc = pl.c("s")
                pl.emit("tensor", lambda e: e.matmul(
                    ps[6][0:64, :], ones_f[64:65, 0:64], rrow[64:65, 0:512],
                    start=True, stop=True),
                    waits=[("s", s_rc), ("s", pl.c("_rbpsfree")),
                           ("v", pl.c("_nrmfree"))], inc="t")
                t_rb = pl.c("t")
                pl.cnt["_rrowfree"] = t_rb
                pl.emit("scalar", lambda e: e.copy(rb_sb[0:64, :], ps[6][0:64, :]),
                        waits=[("t", t_rb), ("v", pl.c("_rbsbv"))], inc="s")
                s_rb = pl.c("s")
                pl.cnt["_rbpsfree"] = s_rb
                pl.cnt["_rbread"] = s_rb
                oap = _sl(attnoutT, pc * 2048 + j * 512,
                          [[attnoutT.ap().ap[0][0], 128], [1, 512]])
                if hi == 0:
                    pl.emit("vector", lambda e, pv=pv, o=oap: e.tensor_tensor(
                        bass.AP(o.tensor, o.offset, o.ap)[0:64, :],
                        pv[0:64, :], rb_sb[0:64, :], op=ALU.mult),
                        waits=[("s", s_rb)], inc="v")
                else:
                    pl.emit("vector", lambda e, pv=pv: e.tensor_tensor(
                        tmpB[0:64, :], pv[0:64, :], rb_sb[0:64, :], op=ALU.mult),
                        waits=[("s", s_rb), ("gd", pl.c("_tmpBfree"))], inc="v")
                    v_tb = pl.c("v")
                    pl.emit("sync", lambda e, o=oap: e.dma_start(
                        out=bass.AP(o.tensor, o.offset, o.ap)[64:128, :],
                        in_=tmpB[0:64, :]),
                        waits=[("v", v_tb)], inc="gd", inc_n=16)
                    pl.cnt["_tmpBfree"] = pl.c("gd")
                pl.cnt["_rbsbv"] = pl.c("v")
            pl.cnt["_pvfree"] = pl.c("v")

        pending = None
        for cc in range(4):
            hA, hB = 8 * g + 2 * cc, 8 * g + 2 * cc + 1
            pc = 4 * g + cc
            for j in range(4):
                nkb = 8 * j + 8
                base_wait = [("v", max(qT_v[(g, cc)], kT_v[(g, cc)], v_vpad)),
                             ("s", s_biasH_g[g])]
                deps = {}
                kperms = {}
                for kb in range(nkb):
                    own = kb < 4 * j + 4
                    kperm = kb if own else 16 + (kb - (4 * j + 4))
                    rel = (kperm - 4 * j) if own else (kperm - 16 - 4 * j)
                    kperms[kb] = kperm
                    sa = ps[kb % 2]
                    sb = ps[2 + kb % 2]
                    w0 = [("s", pl.c(f"_exA{kb % 2}"))] + (base_wait if kb == 0 else [])
                    pl.emit("tensor", lambda e, sa=sa, cc=cc, kp=kperm, j=j: e.matmul(
                        sa[:], _sl(kT, cc * S + kp * 128,
                                   [[kT.ap().ap[0][0], 128], [1, 128]])[0:64, :],
                        _sl(qT, cc * 2048 + j * 512,
                            [[qT.ap().ap[0][0], 128], [1, 512]])[0:64, :],
                        start=True, stop=True, tile_position=(0, 0)),
                        waits=w0, inc="t")
                    t_sa = pl.c("t")
                    w1 = [("s", pl.c(f"_exB{kb % 2}"))] + (base_wait if kb == 0 else [])
                    pl.emit("tensor", lambda e, sb=sb, cc=cc, kp=kperm, j=j: e.matmul(
                        sb[:], _sl(kT, cc * S + kp * 128,
                                   [[kT.ap().ap[0][0], 128], [1, 128]])[64:128, :],
                        _sl(qT, cc * 2048 + j * 512,
                            [[qT.ap().ap[0][0], 128], [1, 512]])[64:128, :],
                        start=True, stop=True, tile_position=(64, 0)),
                        waits=w1, inc="t")
                    t_sb = pl.c("t")
                    pl.emit("scalar", lambda e, sa=sa, kb=kb, hA=hA: e.activation(
                        prA[:, (kb % 2) * 512:(kb % 2 + 1) * 512], sa[:], AF.Exp,
                        bias=biasH[:, hA:hA + 1], scale=0.125),
                        waits=[("t", t_sa), ("t", pl.c(f"_pvA{kb % 2}"))], inc="s")
                    pl.cnt[f"_exA{kb % 2}"] = pl.c("s")
                    pl.emit("scalar", lambda e, sb=sb, kb=kb, hB=hB: e.activation(
                        prB[:, (kb % 2) * 512:(kb % 2 + 1) * 512], sb[:], AF.Exp,
                        bias=biasH[:, hB:hB + 1], scale=0.125),
                        waits=[("t", t_sb), ("t", pl.c(f"_pvB{kb % 2}"))], inc="s")
                    pl.cnt[f"_exB{kb % 2}"] = pl.c("s")
                    s_ex = pl.c("s")
                    dep = ("s", s_ex)
                    nv = False
                    for ci in range(4):
                        mop = None
                        if rel == ci:
                            mop = tri if own else pm
                        elif 0 <= ci < rel:
                            mop = "zero"
                        if mop is None:
                            continue
                        for pr in (prA, prB):
                            col = (kb % 2) * 512 + ci * 128
                            if mop == "zero":
                                pl.emit("vector", lambda e, pr=pr, col=col: e.memset(
                                    pr[:, col:col + 128], 0.0),
                                    waits=[("s", s_ex)], inc="v")
                            else:
                                pl.emit("vector", lambda e, pr=pr, col=col, m=mop:
                                        e.tensor_tensor(
                                    pr[:, col:col + 128], pr[:, col:col + 128],
                                    m[:], op=ALU.mult),
                                    waits=[("s", s_ex), ("xd", xd_consts)], inc="v")
                        nv = True
                    if nv:
                        dep = ("v", pl.c("v"))
                    deps[kb] = dep


                    if kb >= 1:
                        pkb = kb - 1
                        extra = [("v", pl.c("_pvfree"))] if pkb == 0 else []
                        pl.emit("tensor", lambda e, cc=cc, kp=kperms[pkb], kb=pkb,
                                nkb=nkb: e.matmul(
                            ps[4][0:65, :],
                            _sl(vpad, kp * 520 + 2 * cc * 65,
                                [[vpad.ap().ap[0][0], 128], [1, 65]]),
                            prA[:, (kb % 2) * 512:(kb % 2 + 1) * 512],
                            start=(kb == 0), stop=(kb == nkb - 1)),
                            waits=[deps[pkb]] + extra, inc="t")
                        pl.cnt[f"_pvA{pkb % 2}"] = pl.c("t")
                        pl.emit("tensor", lambda e, cc=cc, kp=kperms[pkb], kb=pkb,
                                nkb=nkb: e.matmul(
                            ps[5][0:65, :],
                            _sl(vpad, kp * 520 + (2 * cc + 1) * 65,
                                [[vpad.ap().ap[0][0], 128], [1, 65]]),
                            prB[:, (kb % 2) * 512:(kb % 2 + 1) * 512],
                            start=(kb == 0), stop=(kb == nkb - 1)),
                            waits=[deps[pkb]], inc="t")
                        pl.cnt[f"_pvB{pkb % 2}"] = pl.c("t")
                pkb = nkb - 1
                pl.emit("tensor", lambda e, cc=cc, kp=kperms[pkb], kb=pkb,
                        nkb=nkb: e.matmul(
                    ps[4][0:65, :],
                    _sl(vpad, kp * 520 + 2 * cc * 65,
                        [[vpad.ap().ap[0][0], 128], [1, 65]]),
                    prA[:, (kb % 2) * 512:(kb % 2 + 1) * 512],
                    start=(kb == 0), stop=(kb == nkb - 1)),
                    waits=[deps[pkb]], inc="t")
                pl.cnt[f"_pvA{pkb % 2}"] = pl.c("t")
                pl.emit("tensor", lambda e, cc=cc, kp=kperms[pkb], kb=pkb,
                        nkb=nkb: e.matmul(
                    ps[5][0:65, :],
                    _sl(vpad, kp * 520 + (2 * cc + 1) * 65,
                        [[vpad.ap().ap[0][0], 128], [1, 65]]),
                    prB[:, (kb % 2) * 512:(kb % 2 + 1) * 512],
                    start=(kb == 0), stop=(kb == nkb - 1)),
                    waits=[deps[pkb]], inc="t")
                pl.cnt[f"_pvB{pkb % 2}"] = pl.c("t")
                emit_renorm((pl.c("t"), pc, j))
        pl.cnt["_vpadt"] = pl.c("t")

    if DEBUG:
        pl.emit("sync", lambda e: e.dma_start(out=d_dbg_lnxT, in_=lnxT[:]),
                waits=[("lt", lt_done)], inc="out", inc_n=16)
        pl.emit("sync", lambda e: e.dma_start(out=d_dbg_qT, in_=qT[:]),
                waits=[("v", pl.c("v"))], inc="out", inc_n=16)
        pl.emit("sync", lambda e: e.dma_start(out=d_dbg_kT, in_=kT[:]),
                waits=[("v", pl.c("v"))], inc="out", inc_n=16)
        pl.emit("sync", lambda e: e.dma_start(out=d_dbg_vp, in_=vpad[:]),
                waits=[("v", pl.c("v"))], inc="out", inc_n=16)
        pl.emit("sync", lambda e: e.dma_start(out=d_dbg_bh, in_=biasH[:]),
                waits=[("s", pl.c("s"))], inc="out", inc_n=16)
        pl.emit("sync", lambda e: e.dma_start(out=d_dbg_ao, in_=attnoutT[:]),
                waits=[("v", pl.c("v")), ("xd", pl.c("xd"))],
                inc="out", inc_n=16)
        out_extra = 16 * 34
    else:
        out_extra = 0

    xd_attn = pl.c("xd")
    gd_attn = pl.c("gd")
    v_attn = pl.c("v")
    t_attn = pl.c("t")

    pl.emit("sync", lambda e: e.dma_start(out=bob_s[:], in_=d_bob),
            waits=[("v", v_attn), ("t", t_attn), ("s", pl.c("s"))],
            inc="xd", inc_n=16)
    xd_bob = pl.c("xd")
    for kc in range(8):
        for half in range(2):
            pl.emit("sync", lambda e, kc=kc, h=half: e.dma_start(
                out=ft1[:, 0:512], in_=d_wo[kc * 128:(kc + 1) * 128,
                                            h * 512:(h + 1) * 512]),
                waits=[("s", pl.c("_wofree")), ("v", v_attn), ("t", t_attn)],
                inc="xd", inc_n=16)
            xd_w = pl.c("xd")
            pl.emit("scalar", lambda e, kc=kc, h=half: e.copy(
                wo_bf[:, kc * 1024 + h * 512:kc * 1024 + (h + 1) * 512],
                ft1[:, 0:512]),
                waits=[("xd", xd_w)], inc="s")
            pl.cnt["_wofree"] = pl.c("s")
    s_wo = pl.c("s")

    def qw_emitB(buf, W, xd_w, dstT, ti, dst_nchunks, dst_chunk_w):
        par = ti % 2
        emit_quant(buf, W, fqb, ft1, fc1, fc2, fc3, fmb, fsc, frc,
                   wait_src=[("xd", xd_w)])
        pl.cnt[f"_qs{par}"] = pl.c("s")
        pl.cnt[f"_qv{par}"] = pl.c("v")
        v_q = pl.c("v")
        outap = _sl(dstT, ti * 128,
                    [[dstT.ap().ap[0][0], 128], [dst_chunk_w, dst_nchunks],
                     [1, 128]])
        inap = bass.AP(fqb.ap().tensor, fqb.ap().offset,
                       [[fqb.ap().ap[0][0], 128], [1, W]])
        pl.emit("sync", lambda e, o=outap, i=inap: e.dma_start_transpose(
            out=o, in_=i), waits=[("v", v_q)], inc="tp", inc_n=16)
        pl.cnt["_fqbxd"] = pl.c("tp")

    def quant_weight(dram, nrow_tiles, W, dstT, dst_nchunks, dst_chunk_w):
        pend = None
        for ti in range(nrow_tiles):
            par = ti % 2
            buf = ft2 if par == 0 else ft2b
            pl.emit("sync", lambda e, ti=ti, W=W, dd=dram, b=buf: e.dma_start(
                out=b[:, 0:W], in_=dd[ti * 128:(ti + 1) * 128, :]),
                waits=[("v", pl.c(f"_qv{par}")),
                       ("s", max(pl.c(f"_qs{par}"), s_wo)),
                       ("xd", xd_attn), ("t", t_attn)],
                inc="xd", inc_n=16)
            xd_w = pl.c("xd")
            if pend is not None:
                qw_emitB(*pend)
            pend = (buf, W, xd_w, dstT, ti, dst_nchunks, dst_chunk_w)
        qw_emitB(*pend)

    quant_weight(d_wg, 22, 1024, wgqT, 8, I)
    if DEBUG:
        pl.emit("sync", lambda e: e.dma_start(out=d_dbg_wgq, in_=wgqT[:]),
                waits=[("tp", pl.c("tp"))], inc="out", inc_n=16)
    quant_weight(d_wu, 22, 1024, wuqT, 8, I)
    if DEBUG:
        pl.emit("sync", lambda e: e.dma_start(out=d_dbg_wuq, in_=wuqT[:]),
                waits=[("tp", pl.c("tp"))], inc="out", inc_n=16)
    def wd_emitB(buf, iw, xd_w, ti, i0, pi):
        par = pi % 2
        emit_quant(buf, iw, fqb, ft1, fc1, fc2, fc3, fmb, fsc, frc,
                   wait_src=[("xd", xd_w)])
        pl.cnt[f"_qs{par}"] = pl.c("s")
        pl.cnt[f"_qv{par}"] = pl.c("v")
        v_q = pl.c("v")
        outap = _sl(wdqT, ti * 128 + (i0 // 128) * 1024,
                    [[wdqT.ap().ap[0][0], 128], [1024, iw // 128], [1, 128]])
        inap = bass.AP(fqb.ap().tensor, fqb.ap().offset,
                       [[fqb.ap().ap[0][0], 128], [1, iw]])
        pl.emit("sync", lambda e, o=outap, i=inap, iw=iw: e.dma_start_transpose(
            out=o, in_=i), waits=[("v", v_q)], inc="tp", inc_n=16)
        pl.cnt["_fqbxd"] = pl.c("tp")

    wd_pend = None
    wd_pi = 0
    for ti in range(8):
        for (i0, iw) in ((0, 1024), (1024, 1024), (2048, 768)):
            par = wd_pi % 2
            buf = ft2 if par == 0 else ft2b
            pl.emit("sync", lambda e, ti=ti, i0=i0, iw=iw, b=buf: e.dma_start(
                out=b[:, 0:iw],
                in_=d_wd[ti * 128:(ti + 1) * 128, i0:i0 + iw]),
                waits=[("v", pl.c(f"_qv{par}")), ("s", pl.c(f"_qs{par}")),
                       ("xd", xd_attn), ("t", t_attn)],
                inc="xd", inc_n=16)
            xd_w = pl.c("xd")
            if wd_pend is not None:
                wd_emitB(*wd_pend)
            wd_pend = (buf, iw, xd_w, ti, i0, wd_pi)
            wd_pi += 1
    wd_emitB(*wd_pend)
    v_wq = pl.c("v")
    s_wq = pl.c("s")
    if DEBUG:
        pl.emit("sync", lambda e: e.dma_start(out=d_dbg_wdq, in_=wdqT[:]),
                waits=[("tp", pl.c("tp"))], inc="out", inc_n=16)
    xd_wq = pl.c("xd")
    tp_wq = pl.c("tp")

    out_n = 0
    pl.cnt["_zfree"] = v_attn
    pl.cnt["_gufree"] = pl.c("v")
    pl.cnt["_yfree"] = v_attn
    for tc in range(NQB):
        pl.emit("sync", lambda e, tc=tc: e.dma_start(
            out=xres[:], in_=d_x[tc * 128:(tc + 1) * 128, :]),
            waits=[("v", max(pl.c("_xrfree"), v_wq)), ("s", s_wq),
                   ("xd", xd_wq)], inc="xd", inc_n=16)
        xd_xr = pl.c("xd")
        for half in range(2):
            pb = ps[half]
            for kc in range(8):
                w0 = ([("s", s_wo), ("v", pl.c("_zfree")), ("xd", xd_attn),
                       ("gd", gd_attn),
                       ("s", pl.c("_exA0")), ("s", pl.c("_exA1")),
                       ("s", pl.c("_exB0")), ("s", pl.c("_exB1"))]
                      if kc == 0 else [])
                pl.emit("tensor", lambda e, pb=pb, kc=kc, tc=tc, h=half: e.matmul(
                    pb[:], _sl(attnoutT, kc * 2048 + tc * 128,
                               [[attnoutT.ap().ap[0][0], 128], [1, 128]]),
                    wo_bf[:, kc * 1024 + h * 512:kc * 1024 + (h + 1) * 512],
                    start=(kc == 0), stop=(kc == 7)), waits=w0, inc="t")
            t_z = pl.c("t")
            pl.emit("vector", lambda e, pb=pb, h=half: e.tensor_tensor(
                x2[:, h * 512:(h + 1) * 512], pb[:],
                xres[:, h * 512:(h + 1) * 512], op=ALU.add),
                waits=[("t", t_z), ("xd", xd_xr), ("v", pl.c("_x2free"))], inc="v")
        pl.emit("vector", lambda e: e.tensor_tensor(x2[:], x2[:], bob_s[:],
                                                    op=ALU.add),
                waits=[("xd", xd_bob)], inc="v")
        pl.cnt["_zfree"] = pl.c("v")
        pl.cnt["_xrfree"] = pl.c("v")
        v_x2 = pl.c("v")
        if DEBUG and tc == 0:
            pl.emit("sync", lambda e: e.dma_start(out=d_dbg_x2, in_=x2[:]),
                    waits=[("v", v_x2)], inc="out", inc_n=16)
        pl.emit("vector", lambda e: e.reduce_sum(fstats[:, 0:1], x2[:], axis=AXX),
                waits=[("v", v_x2), ("s", pl.c("_fst23"))], inc="v")
        pl.emit("scalar", lambda e: e.activation(ft1[:, 0:1024], x2[:], AF.Square),
                waits=[("v", v_x2), ("v", pl.c("v"))], inc="s")
        s_sq = pl.c("s")
        pl.emit("vector", lambda e: e.reduce_sum(fstats[:, 1:2], ft1[:, 0:1024],
                                                 axis=AXX),
                waits=[("s", s_sq)], inc="v")
        v_red = pl.c("v")
        pl.emit("scalar", lambda e: e.mul(fstats[:, 2:3], fstats[:, 0:1], 1.0 / E),
                waits=[("v", v_red)], inc="s")
        pl.emit("scalar", lambda e: e.mul(fstats[:, 3:4], fstats[:, 1:2], 1.0 / E),
                inc="s")
        pl.cnt["_fst23"] = pl.c("s")
        s_m = pl.c("s")
        pl.emit("gpsimd", lambda e: e.tensor_tensor(
            fstats[:, 4:5], fstats[:, 2:3], fstats[:, 2:3], op=ALU.mult),
            waits=[("s", s_m)], inc="g")
        pl.emit("gpsimd", lambda e: e.tensor_tensor(
            fstats[:, 4:5], fstats[:, 3:4], fstats[:, 4:5], op=ALU.subtract),
            inc="g")
        g_v = pl.c("g")
        pl.emit("scalar", lambda e: e.activation(fstats[:, 5:6], fstats[:, 4:5],
                                                 AF.Ln, bias=epsap[:]),
                waits=[("g", g_v)], inc="s")
        pl.emit("scalar", lambda e: e.activation(fstats[:, 6:7], fstats[:, 5:6],
                                                 AF.Exp, scale=-0.5), inc="s")
        s_r = pl.c("s")
        pl.emit("vector", lambda e: e.tensor_scalar(
            ft2[:, 0:1024], x2[:], fstats[:, 2:3], fstats[:, 6:7],
            ALU.subtract, ALU.mult),
            waits=[("s", s_r), ("out", pl.c("_ft2out"))], inc="v")
        emit_quant(ft2, 1024, fqb, ft1, fc1, fc2, fc3, fmb, fsc, frc)
        v_xq = pl.c("v")
        if DEBUG and tc == 0:
            pl.emit("sync", lambda e: e.dma_start(out=d_dbg_ln2, in_=ft2[:, 0:1024]),
                    waits=[("v", v_xq)], inc="out", inc_n=16)
            pl.emit("sync", lambda e: e.dma_start(out=d_dbg_xq, in_=fqb[:, 0:1024]),
                    waits=[("v", v_xq)], inc="out", inc_n=16)
        outap = bass.AP(xqT.ap().tensor, xqT.ap().offset,
                        [[xqT.ap().ap[0][0], 128], [128, 8], [1, 128]])
        inap = bass.AP(fqb.ap().tensor, fqb.ap().offset,
                       [[fqb.ap().ap[0][0], 128], [1, 1024]])
        pl.emit("sync", lambda e, o=outap, i=inap: e.dma_start_transpose(
            out=o, in_=i), waits=[("v", v_xq), ("t", pl.c("_xqTt"))],
            inc="tp", inc_n=16)
        pl.cnt["_fqbxd"] = pl.c("tp")
        tp_xq = pl.c("tp")
        if DEBUG and tc == 0:
            pl.emit("sync", lambda e: e.dma_start(out=d_dbg_xqT, in_=xqT[:]),
                    waits=[("tp", tp_xq)], inc="out", inc_n=16)
        pieces = [(0, 1024), (1024, 1024), (2048, 768)]
        for (i0, iw) in pieces:
            ntiles = (iw + 511) // 512
            for (wsrc, pbase) in ((wgqT, 2), (wuqT, 4)):
                for nt in range(ntiles):
                    nw = min(512, iw - nt * 512)
                    pdst = ps[pbase + nt]
                    for kc in range(8):
                        w0 = ([("xd", xd_wq), ("tp", max(tp_xq, tp_wq)),
                               ("v", pl.c("_gufree"))]
                              if kc == 0 else [])
                        pl.emit("tensor", lambda e, pdst=pdst, kc=kc, i0=i0, nt=nt,
                                nw=nw, wsrc=wsrc: e.matmul(
                            bass.AP(pdst.ap().tensor, pdst.ap().offset,
                                    [[pdst.ap().ap[0][0], 128], [1, nw]]),
                            xqT[:, kc * 128:(kc + 1) * 128],
                            _sl(wsrc, kc * I + i0 + nt * 512,
                                [[wsrc.ap().ap[0][0], 128], [1, nw]]),
                            start=(kc == 0), stop=(kc == 7)), waits=w0, inc="t")
            t_gu = pl.c("t")
            for nt in range(ntiles):
                nw = min(512, iw - nt * 512)
                pl.emit("scalar", lambda e, nt=nt, nw=nw: e.activation(
                    ft1[:, nt * 512:nt * 512 + nw],
                    bass.AP(ps[2 + nt].ap().tensor, ps[2 + nt].ap().offset,
                            [[ps[2 + nt].ap().ap[0][0], 128], [1, nw]]), AF.Silu),
                    waits=[("t", t_gu), ("v", pl.c("v"))], inc="s")
                s_sil = pl.c("s")
                pl.emit("vector", lambda e, nt=nt, nw=nw: e.tensor_tensor(
                    ft2[:, nt * 512:nt * 512 + nw],
                    ft1[:, nt * 512:nt * 512 + nw],
                    bass.AP(ps[4 + nt].ap().tensor, ps[4 + nt].ap().offset,
                            [[ps[4 + nt].ap().ap[0][0], 128], [1, nw]]),
                    op=ALU.mult),
                    waits=[("s", s_sil)], inc="v")
            pl.cnt["_gufree"] = pl.c("v")
            emit_quant(ft2, iw, fqb, ft1, fc1, fc2, fc3, fmb, fsc, frc)
            v_hq = pl.c("v")
            nch = iw // 128
            outap = _sl(hqT, (i0 // 128) * 128,
                        [[hqT.ap().ap[0][0], 128], [128, nch], [1, 128]])
            inap = bass.AP(fqb.ap().tensor, fqb.ap().offset,
                           [[fqb.ap().ap[0][0], 128], [1, iw]])
            pl.emit("sync", lambda e, o=outap, i=inap: e.dma_start_transpose(
                out=o, in_=i), waits=[("v", v_hq), ("t", pl.c("_hqTt"))],
                inc="tp", inc_n=16)
            pl.cnt["_fqbxd"] = pl.c("tp")
        tp_hq = pl.c("tp")
        if DEBUG and tc == 0:
            pl.emit("sync", lambda e: e.dma_start(out=d_dbg_hqT, in_=hqT[:]),
                    waits=[("tp", tp_hq)], inc="out", inc_n=16)
        for half in range(2):
            yb = ps[6 + half]
            for kc in range(22):
                w0 = ([("tp", tp_hq), ("v", pl.c("_yfree")),
                       ("s", pl.c("_rbpsfree"))] if kc == 0 else [])
                pl.emit("tensor", lambda e, yb=yb, kc=kc, h=half: e.matmul(
                    yb[:], hqT[:, kc * 128:(kc + 1) * 128],
                    _sl(wdqT, kc * 1024 + h * 512,
                        [[wdqT.ap().ap[0][0], 128], [1, 512]]),
                    start=(kc == 0), stop=(kc == 21)), waits=w0, inc="t")
            t_y = pl.c("t")
            if half == 1:
                pl.cnt["_hqTt"] = t_y
                pl.cnt["_xqTt"] = t_y
            pl.emit("vector", lambda e, yb=yb, h=half: e.tensor_tensor(
                ft2[:, h * 512:(h + 1) * 512], yb[:],
                x2[:, h * 512:(h + 1) * 512], op=ALU.add),
                waits=[("t", t_y)], inc="v")
        pl.cnt["_yfree"] = pl.c("v")
        pl.cnt["_x2free"] = pl.c("v")
        v_out = pl.c("v")
        pl.emit("sync", lambda e, tc=tc: e.dma_start(
            out=d_out[tc * 128:(tc + 1) * 128, :], in_=ft2[:, 0:1024]),
            waits=[("v", v_out)], inc="out", inc_n=16)
        out_n += 16
        pl.cnt["_ft2out"] = pl.c("out")

    pl.emit("gpsimd", lambda e: e.engine_nop(), waits=[("out", out_n + out_extra)])

    import contextlib
    with contextlib.ExitStack() as stk:
        sems = {s: stk.enter_context(nc.semaphore(s)) for s in SEMS}
        block = stk.enter_context(nc.Block())

        def mk(engname):
            def run(eng):
                for fn, waits, inc, inc_n in pl.items[engname]:
                    for s, v in waits:
                        eng.wait_ge(sems[s], v)
                    inst = fn(eng)
                    if inc is not None:
                        inst.then_inc(sems[inc], inc_n)
            return run

        block.sync(mk("sync"))
        block.tensor(mk("tensor"))
        block.vector(mk("vector"))
        block.scalar(mk("scalar"))
        block.gpsimd(mk("gpsimd"))
    return nc


_NC_CACHE = {}


def _host_constants():
    inv = 1.0 / (10000.0 ** (np.arange(0, DH, 2, dtype=np.float64) / DH))
    pos = np.arange(S, dtype=np.float64)
    ang = pos[None, :] * inv[:, None]
    cos, sin = np.cos(ang), np.sin(ang)
    chi = cos.astype(np.float32)
    clo = (cos - chi).astype(np.float32)
    shi = sin.astype(np.float32)
    slo = (sin - shi).astype(np.float32)
    CHI = np.tile(chi, (4, 1)).astype(np.float32)
    CLO = np.tile(clo, (4, 1)).astype(np.float32)
    SHI = np.concatenate([-shi, shi, -shi, shi], 0).astype(np.float32)
    SLO = np.concatenate([-slo, slo, -slo, slo], 0).astype(np.float32)
    rk = np.arange(128)[:, None]
    rq = np.arange(128)[None, :]
    TRI = (rk <= rq).astype(ml_dtypes.bfloat16)
    return CHI, CLO, SHI, SLO, TRI


def kernel(L0_attn_x, L0_attn_Wq, L0_attn_bq, L0_attn_Wkv, L0_attn_bkv,
           L0_attn_Wo, L0_attn_bo, L0_ffn_w_gate, L0_ffn_w_up, L0_ffn_w_down):
    x = np.asarray(L0_attn_x, dtype=np.float32)
    wq = np.ascontiguousarray(np.asarray(L0_attn_Wq, dtype=np.float32))
    bq = np.asarray(L0_attn_bq, dtype=np.float32)
    wkv = np.ascontiguousarray(np.asarray(L0_attn_Wkv, dtype=np.float32))
    bkv = np.asarray(L0_attn_bkv, dtype=np.float32)
    wo = np.ascontiguousarray(np.asarray(L0_attn_Wo, dtype=np.float32))
    bo = np.asarray(L0_attn_bo, dtype=np.float32)
    wg = np.ascontiguousarray(np.asarray(L0_ffn_w_gate, dtype=np.float32))
    wu = np.ascontiguousarray(np.asarray(L0_ffn_w_up, dtype=np.float32))
    wd = np.ascontiguousarray(np.asarray(L0_ffn_w_down, dtype=np.float32))

    CHI, CLO, SHI, SLO, TRI = _host_constants()
    bq_s = np.ascontiguousarray(bq.reshape(8, 128).T)
    bkv_s = np.ascontiguousarray(bkv.reshape(16, 128).T)
    bvb = np.ascontiguousarray(np.broadcast_to(bkv[E:2 * E][None, :], (128, E)))
    bob = np.ascontiguousarray(np.broadcast_to(bo[None, :], (128, E)))

    in_maps = []
    for c in range(8):
        b, p = c // 2, c % 2
        blocks = [2 * i + p for i in range(16)] + \
                 [2 * i + (1 - p) for i in range(16)]
        colperm = np.concatenate(
            [np.arange(g * 128, (g + 1) * 128) for g in blocks])
        xb = x[b].reshape(32, 128, E)[blocks].reshape(S, E)
        pmv = np.full((128, 128), 1.0 if p == 1 else 0.0, ml_dtypes.bfloat16)
        in_maps.append({
            "x": np.ascontiguousarray(xb),
            "wq": wq, "wkv": wkv, "wo": wo, "wg": wg, "wu": wu, "wd": wd,
            "bq": bq_s, "bkv": bkv_s, "bvb": bvb, "bob": bob,
            "chi": np.ascontiguousarray(CHI[:, colperm]),
            "clo": np.ascontiguousarray(CLO[:, colperm]),
            "shi": np.ascontiguousarray(SHI[:, colperm]),
            "slo": np.ascontiguousarray(SLO[:, colperm]),
            "tri": TRI, "pm": pmv,
        })

    if "nc" not in _NC_CACHE:
        _NC_CACHE["nc"] = build_core()
    nc = _NC_CACHE["nc"]
    res = run_bass_kernel_spmd(nc, in_maps, core_ids=list(range(8)))

    out = np.empty((B, S, E), dtype=np.float32)
    for c in range(8):
        b, p = c // 2, c % 2
        co = np.asarray(res.results[c]["out"], dtype=np.float32).reshape(16, 128, E)
        out.reshape(B, 32, 128, E)[b, p::2] = co
    if DEBUG:
        globals()["last_debug"] = res.results
    return out
